# version 11
# baseline (speedup 1.0000x reference)
import contextlib
import numpy as np
import ml_dtypes

import concourse.bass as bass
import concourse.mybir as mybir
from concourse.bass_utils import run_bass_kernel_spmd

F32 = mybir.dt.float32
BF16 = mybir.dt.bfloat16
I16 = mybir.dt.int16
U8 = mybir.dt.uint8
Alu = mybir.AluOpType
Act = mybir.ActivationFunctionType
AX = mybir.AxisListType.X

D = 2048
T = 2048
TH = 1024
NT = 16
H = 16
MLP = 16384
EPSF = float(np.finfo(np.float32).eps)
SCL = float(128.0 ** -0.5)

ENGS = ("sync", "vector", "scalar", "tensor", "gpsimd")
C_M2, C_M6, C_MH, C_PH = 0, 1, 2, 3
SY = "sync"


class Rec:
    def __init__(self, nc):
        self.nc = nc
        self.cmds = {e: [] for e in ENGS}
        self.es = contextlib.ExitStack()
        self.sems = {}
        self.val = {}
        self.wset = {e: {} for e in ENGS}
        self.rset = {e: set() for e in ENGS}

    def sem(self, name):
        if name not in self.sems:
            self.sems[name] = self.es.enter_context(self.nc.semaphore(name))
            self.val[name] = 0
        return self.sems[name]

    def op(self, eng, meth, *a, inc=None, **k):
        semtup = None
        if inc is not None:
            sname, n = inc
            semh = self.sem(sname)
            self.val[sname] += n
            semtup = (semh, n)
        if eng in ("vector", "scalar") and meth not in ("dma_start", "sem_inc"):
            if meth == "drain":
                self.wset[eng].clear()
                self.rset[eng].clear()
                self.cmds[eng].append(("op", meth, a, k, semtup))
                return self.val[inc[0]] if inc else None
            outn = self._apname(a[0]) if a else None
            outoff = getattr(a[0], "offset", None) if a else None
            fs = getattr(a[0], "free_size", None) if a else None
            if callable(fs):
                fs = fs()
            risky = (meth == "memset" or not isinstance(fs, int) or fs < 1024
                     or not isinstance(outoff, int))
            ins = []
            for x in list(a[1:]) + list(k.values()):
                n = self._apname(x)
                if n:
                    ins.append((n, getattr(x, "offset", None)))
            need = False
            for n, off in ins:
                w = self.wset[eng].get(n)
                if w is None:
                    continue
                woff, wrisky = w
                if wrisky or meth == "tensor_tensor_scan" \
                        or not isinstance(off, int) or off != woff:
                    need = True
                    break
            if need:
                self.cmds[eng].append(("op", "drain", (), {}, None))
                self.wset[eng].clear()
                self.rset[eng].clear()
            if outn:
                self.wset[eng][outn] = (outoff, risky)
            for n, _ in ins:
                self.rset[eng].add(n)
        self.cmds[eng].append(("op", meth, a, k, semtup))
        return self.val[inc[0]] if inc else None

    @staticmethod
    def _apname(x):
        n = getattr(x, "name", None)
        if callable(n):
            try:
                n = n()
            except Exception:
                return None
        return n if isinstance(n, str) else None

    def mark(self, name):
        for e in ENGS:
            self.cmds[e].append(("scope", name))

    def barrier(self, engs=ENGS):
        b = self.sem("bar")
        for e in engs:
            self.cmds[e].append(("op", "sem_inc", (b, 1), {}, None))
            self.val["bar"] += 1
        tgt = self.val["bar"]
        for e in engs:
            self.cmds[e].append(("wait", b, tgt))

    def inc_last(self, eng, sname, n=1):
        semh = self.sem(sname)
        self.val[sname] += n
        for i in range(len(self.cmds[eng]) - 1, -1, -1):
            c = self.cmds[eng][i]
            if c[0] == "op":
                assert c[4] is None, "last op already has an inc"
                self.cmds[eng][i] = (c[0], c[1], c[2], c[3], (semh, n))
                return self.val[sname]
        raise RuntimeError("no op to attach inc")

    def wait(self, eng, sname, value=None):
        semh = self.sem(sname)
        v = self.val[sname] if value is None else value
        self.cmds[eng].append(("wait", semh, v))
        return v

    def sb(self, name, shape, dtype, stack=None):
        return (stack or self.es).enter_context(self.nc.sbuf_tensor(name, shape, dtype))

    def psum(self, name, shape, dtype=F32, stack=None):
        return (stack or self.es).enter_context(self.nc.psum_tensor(name, shape, dtype))

    def dma(self, eng, out, in_, sname, transpose=False):
        if transpose:
            return self.op(eng, "dma_start", inc=(sname, 16), out=out, in_=in_,
                           transpose=True)
        return self.op(eng, "dma_start", inc=(sname, 16), out=out, in_=in_)

    def build(self):
        nc = self.nc
        with nc.Block() as block:
            def make(eng_name):
                def runner(eng):
                    cur = None
                    for c in self.cmds[eng_name]:
                        if c[0] == "op":
                            _, meth, a, k, semtup = c
                            ins = getattr(eng, meth)(*a, **k)
                            if semtup is not None:
                                ins.then_inc(*semtup)
                        elif c[0] == "scope":
                            if cur is not None:
                                nc.leave_named_scope(cur[0], cur[1], False)
                            sid, _ = nc.enter_named_scope(c[1], False)
                            cur = (c[1], sid)
                        else:
                            eng.wait_ge(c[1], c[2])
                    if cur is not None:
                        nc.leave_named_scope(cur[0], cur[1], False)
                return runner
            block.sync(make("sync"))
            block.vector(make("vector"))
            block.scalar(make("scalar"))
            block.tensor(make("tensor"))
            block.gpsimd(make("gpsimd"))
        return nc


class KB:

    def __init__(self, expose=()):
        self.nc = bass.Bass("TRN2", target_bir_lowering=False, debug=False,
                            detect_race_conditions=False)
        self.r = Rec(self.nc)
        self.expose = set(expose)
        self._declare_io()
        self._declare_shared()

    def din(self, name, shape, dtype):
        return self.nc.dram_tensor(name, shape, dtype, kind="ExternalInput")

    def dscr(self, name, shape, dtype):
        kind = "ExternalOutput" if name in self.expose else "Internal"
        return self.nc.dram_tensor(name, shape, dtype, kind=kind)

    def _declare_io(self):
        d = self.din
        self.x_in = d("x_fm", [D, T], F32)
        self.qkvT = d("qkvT", [D, 3 * D], F32)
        self.projT = d("projT", [D, D], F32)
        self.ffn0T = d("ffn0T", [D, D], F32)
        self.shiftT = d("shiftT", [D, D], F32)
        self.wkvT = d("wkvT", [D, 128], F32)
        self.woutT = d("woutT", [64, D], F32)
        self.nvec = d("nvec", [128, 80], F32)
        self.gvec = d("gvec", [128, 16], F32)
        self.svec = d("svec", [1, 16], F32)
        self.tdvec = d("tdvec", [64, 2], F32)
        self.gp_u8 = d("gp_u8", [MLP, 1024], U8)
        self.up_u8 = d("up_u8", [MLP, 1024], U8)
        self.dn_u8 = d("dn_u8", [D, 8192], U8)
        self.gsc = d("gsc", [MLP, 64], F32)
        self.usc = d("usc", [MLP, 64], F32)
        self.dsc = d("dsc", [D, 512], F32)
        self.cst = d("cst", [128, 8], F32)
        self.idnb = d("idnb", [128, 128], BF16)
        self.idnf = d("idnf", [128, 128], F32)
        self.onesM = d("onesM", [128, 128], F32)
        self.diffT = d("diffT", [128, 128], F32)
        self.triT = d("triT", [128, 128], F32)
        self.permM = d("permM", [128, 128], BF16)
        self.niotaR = d("niotaR", [1, 128], F32)
        self.iotaC = d("iotaC", [128, 1], F32)
        self.pcol = d("pcol", [128, 2], F32)
        self.out_t = self.nc.dram_tensor("out", [D, TH], F32, kind="ExternalOutput")
        s = self.dscr
        self.xres1 = s("xres1", [D, T], F32)
        self.xres2 = s("xres2", [D, T], F32)
        self.xres3 = s("xres3", [D, T], F32)
        self.o0 = s("o0", [D, T], F32)
        self.hbs = s("hbs", [D, T], F32)
        self.xshs = s("xshs", [D, T], F32)
        self.vg1s = s("vg1s", [D, T], F32)
        self.wdg = s("wdg", [MLP, D], BF16)
        self.wdu = s("wdu", [MLP, D], BF16)
        self.wdd = s("wdd", [D, MLP], BF16)

    def _declare_shared(self):
        r = self.r
        self.csts = r.sb("csts", [128, 8], F32)
        self.idbs = r.sb("idbs", [128, 128], BF16)
        self.idfs = r.sb("idfs", [128, 128], F32)
        self.onesMs = r.sb("onesMs", [128, 128], F32)
        self.diffs = r.sb("diffs", [128, 128], F32)
        self.tris = r.sb("tris", [128, 128], F32)
        self.perms = r.sb("perms", [128, 128], BF16)
        self.niotas = r.sb("niotas", [1, 128], F32)
        self.iotacs = r.sb("iotacs", [128, 1], F32)
        self.pcols = r.sb("pcols", [128, 2], F32)
        self.nvs = r.sb("nvs", [128, 80], F32)
        self.gsig = r.sb("gsig", [128, 16], F32)
        self.gsig1m = r.sb("gsig1m", [128, 16], F32)
        self.w12s = r.sb("w12s", [128, 16], F32)
        self.w1sqb = r.sb("w1sqb", [128, 16], BF16)
        self.stw16 = r.sb("stw16", [128, 32], BF16)
        self.scol = r.sb("scol", [128, 16], F32)
        self.svrow = r.sb("svrow", [1, 16], F32)
        self.tds = r.sb("tds", [64, 2], F32)
        self.wcolv = r.sb("wcolv", [64, 1], F32)
        self.ucolv = r.sb("ucolv", [64, 1], F32)
        self.onescb = r.sb("onescb", [128, 1], BF16)
        self.onescf = r.sb("onescf", [128, 1], F32)
        self.qsc = r.sb("qsc", [128, 4], F32)
        self.isc = r.sb("isc", [128, 4], F32)
        self.rowp = r.sb("rowp", [128, T], F32)
        self.statr = r.sb("statr", [1, 2 * T], F32)
        self.RB = r.sb("RB", [128, T], F32)
        self.mRB = r.sb("mRB", [128, T], F32)
        self.ost_es = contextlib.ExitStack()
        self.ost = r.sb("ost", [1, 2 * T], F32, self.ost_es)

    def bcast_row(self, src_t, src_off, dst, pst, tok=T):
        r = self.r
        vsv = r.op("vector", "tensor_copy", self.rowp[0:1, 0:tok],
                   bass.AP(src_t, src_off, [[2 * T, 1], [1, tok]]), inc=("vs", 1))
        r.wait("tensor", "vs", vsv)
        for tc in range(tok // 512):
            sl = slice(tc * 512, (tc + 1) * 512)
            r.op("tensor", "matmul", pst[:, sl], self.onesMs[:],
                 self.rowp[:, sl], start=True, stop=True)
        tsv = r.inc_last("tensor", "ts")
        r.wait("vector", "ts", tsv)
        r.op("vector", "tensor_copy", dst[:, 0:tok], pst[:, 0:tok], inc=("vs", 1))

    def ln_stats(self, src_t, ps1, ps2, xt, xbt, sqt):
        r = self.r
        cons = [r.val["vs"]] * 2
        scons = [r.val["ss"]] * 2
        tcons = [r.val["ts"]] * 2
        for kt in range(NT):
            b = (kt % 2) * T
            r.wait(SY, "vs", cons[kt % 2])
            r.wait(SY, "ss", scons[kt % 2])
            ldx = r.dma(SY, xt[:, b:b + T], src_t.ap()[kt * 128:(kt + 1) * 128, :],
                        "ld")
            r.wait("vector", "ld", ldx)
            r.wait("vector", "ts", tcons[kt % 2])
            vsv = r.op("vector", "tensor_copy", xbt[:, b:b + T], xt[:, b:b + T],
                       inc=("vs", 1))
            cons[kt % 2] = vsv
            r.wait("scalar", "vs", vsv)
            r.wait("scalar", "ts", tcons[kt % 2])
            ssv = r.op("scalar", "activation", sqt[:, b:b + T], xt[:, b:b + T],
                       Act.Square, inc=("ss", 1))
            scons[kt % 2] = ssv
            r.wait("tensor", "vs", vsv)
            r.wait("tensor", "ss", ssv)
            for tc in range(4):
                sl = slice(tc * 512, (tc + 1) * 512)
                r.op("tensor", "matmul", ps1[0:1, sl], self.onescb[:],
                     xbt[:, b + tc * 512:b + (tc + 1) * 512],
                     start=(kt == 0), stop=(kt == NT - 1))
                r.op("tensor", "matmul", ps2[0:1, sl], self.onescb[:],
                     sqt[:, b + tc * 512:b + (tc + 1) * 512],
                     start=(kt == 0), stop=(kt == NT - 1))
            tcons[kt % 2] = r.inc_last("tensor", "ts")

    def ln_finalize(self, ps1, ps2, eps_ln, rms_eps, tok=T):
        r = self.r
        st = self.statr
        r.wait("vector", "ts")
        r.op("vector", "tensor_scalar", st[0:1, 0:tok], ps1[0:1, 0:tok], 1.0 / D,
             None, Alu.mult)
        r.op("vector", "tensor_scalar", st[0:1, T:T + tok], ps2[0:1, 0:tok], 1.0 / D,
             None, Alu.mult)
        r.op("vector", "tensor_tensor", self.rowp[0:1, 0:tok], st[0:1, 0:tok],
             st[0:1, 0:tok], Alu.mult)
        r.op("vector", "tensor_tensor", st[0:1, T:T + tok], st[0:1, T:T + tok],
             self.rowp[0:1, 0:tok], Alu.subtract)
        if rms_eps is None:
            r.op("vector", "tensor_scalar", st[0:1, T:T + tok], st[0:1, T:T + tok], 1.0,
                 eps_ln, Alu.mult, Alu.add)
        else:
            r.op("vector", "tensor_scalar", st[0:1, T:T + tok], st[0:1, T:T + tok],
                 1.0 + rms_eps, rms_eps * eps_ln, Alu.mult, Alu.add)
        r.op("vector", "reciprocal", self.rowp[0:1, 0:tok], st[0:1, T:T + tok], inc=("vs", 1))
        r.wait("scalar", "vs")
        r.op("scalar", "activation", st[0:1, T:T + tok], self.rowp[0:1, 0:tok], Act.Sqrt,
             inc=("ss", 1))
        r.wait("vector", "ss")
        r.op("vector", "tensor_tensor", st[0:1, 0:tok], st[0:1, 0:tok],
             st[0:1, T:T + tok], Alu.mult, inc=("vs", 1))

    def epilogue_zero(self):
        r = self.r
        stf = contextlib.ExitStack()
        z = r.sb("zfin", [128, TH], F32, stf)
        r.op("vector", "memset", z[:], 0.0, inc=("vs", 1))
        r.wait(SY, "vs")
        for mt in range(NT):
            r.dma(SY, self.out_t.ap()[mt * 128:(mt + 1) * 128, :], z[:], "so")
        r.wait(SY, "so")
        stf.close()

    def p0(self):
        r = self.r
        for dst, src in ((self.csts, self.cst), (self.idbs, self.idnb),
                         (self.idfs, self.idnf), (self.onesMs, self.onesM),
                         (self.diffs, self.diffT), (self.tris, self.triT),
                         (self.perms, self.permM), (self.niotas, self.niotaR),
                         (self.iotacs, self.iotaC), (self.pcols, self.pcol),
                         (self.nvs, self.nvec), (self.gsig, self.gvec),
                         (self.svrow, self.svec), (self.tds, self.tdvec)):
            r.dma(SY, dst[:], src.ap(), "ld")
        ld0 = r.val["ld"]
        for e in ("vector", "scalar", "tensor"):
            r.wait(e, "ld", ld0)
        r.op("vector", "memset", self.onescb[:], 1.0)
        r.op("vector", "memset", self.onescf[:], 1.0)
        r.op("vector", "memset", self.rowp[:], 0.0)
        r.op("vector", "memset", self.stw16[:], 1.0)
        r.op("vector", "tensor_tensor", self.w12s[:], self.nvs[:, 16:32],
             self.nvs[:, 32:48], Alu.mult, inc=("vs", 1))
        r.wait("scalar", "vs")
        r.op("scalar", "activation", self.w1sqb[:], self.nvs[:, 16:32], Act.Square)
        r.op("scalar", "activation", self.gsig[:], self.gsig[:], Act.Sigmoid)
        r.op("scalar", "activation", self.wcolv[:], self.tds[:, 0:1], Act.Exp)
        r.op("scalar", "activation", self.ucolv[:], self.tds[:, 1:2], Act.Exp,
             inc=("ss", 1))
        r.wait("vector", "ss")
        r.op("vector", "tensor_scalar", self.gsig1m[:], self.gsig[:], -1.0, 1.0,
             Alu.mult, Alu.add)
        r.op("vector", "tensor_copy", bass.AP(self.stw16, 1, [[32, 128], [2, 16]]),
             self.w1sqb[:])
        r.op("vector", "tensor_copy", self.rowp[0:1, 0:16], self.svrow[:],
             inc=("vs", 1))
        stp = contextlib.ExitStack()
        psP0 = r.psum("psP0", [128, 512], F32, stp)
        r.wait("tensor", "vs")
        r.op("tensor", "matmul", psP0[:, 0:16], self.onesMs[:], self.rowp[:, 0:16],
             start=True, stop=True)
        r.inc_last("tensor", "ts")
        r.wait("vector", "ts")
        r.op("vector", "tensor_copy", self.scol[:], psP0[:, 0:16])
        r.op("vector", "memset", self.rowp[0:1, 0:16], 0.0, inc=("vs", 1))

        st0 = contextlib.ExitStack()
        wst = r.sb("wst0", [128, 2 * 6144], F32, st0)
        acc4 = r.sb("acc40", [128, 4], F32, st0)
        red49 = r.sb("red490", [128, 49], F32, st0)
        r.op("vector", "memset", acc4[:], 0.0, inc=("vs", 1))
        cons = [r.val["vs"]] * 2
        for wi, (wt, wcols) in enumerate(((self.qkvT, 6144), (self.projT, 2048),
                                          (self.ffn0T, 2048))):
            for kt in range(NT):
                b = (kt % 2) * 6144
                r.wait(SY, "vs", cons[kt % 2])
                ldw = r.dma(SY, wst[:, b:b + wcols],
                            wt.ap()[kt * 128:(kt + 1) * 128, :], "ld")
                r.wait("vector", "ld", ldw)
                nch = wcols // 128
                r.op("vector", "tensor_reduce", red49[:, 0:nch],
                     bass.AP(wst, b, [[2 * 6144, 128], [128, nch], [1, 128]]),
                     AX, Alu.add, True)
                r.op("vector", "tensor_reduce", red49[:, 48:49], red49[:, 0:nch],
                     AX, Alu.add)
                cons[kt % 2] = r.op("vector", "tensor_tensor", acc4[:, wi:wi + 1],
                                    acc4[:, wi:wi + 1], red49[:, 48:49], Alu.add,
                                    inc=("vs", 1))
        r.wait("tensor", "vs")
        r.op("tensor", "matmul", psP0[0:1, 32:36], self.onescf[:], acc4[:],
             start=True, stop=True)
        r.inc_last("tensor", "ts")
        r.wait("vector", "ts")
        r.op("vector", "tensor_copy", self.rowp[0:1, 32:36], psP0[0:1, 32:36],
             inc=("vs", 1))
        r.wait("tensor", "vs")
        r.op("tensor", "matmul", psP0[:, 36:40], self.onesMs[:],
             self.rowp[:, 32:36], start=True, stop=True)
        r.inc_last("tensor", "ts")
        r.wait("vector", "ts")
        r.op("vector", "tensor_scalar", acc4[:, 0:1], psP0[:, 36:37],
             1.0 / (2048.0 * 6144.0), 1e-5, Alu.mult, Alu.max)
        r.op("vector", "tensor_scalar", acc4[:, 1:2], psP0[:, 37:38],
             1.0 / (2048.0 * 2048.0), 1e-5, Alu.mult, Alu.max)
        r.op("vector", "tensor_scalar", acc4[:, 2:3], psP0[:, 38:39],
             1.0 / (2048.0 * 2048.0), 1e-8, Alu.mult, Alu.max)
        r.op("vector", "reciprocal", self.qsc[:, 0:3], acc4[:, 0:3])
        r.op("vector", "tensor_scalar", self.isc[:, 0:3], acc4[:, 0:3], 0.5, None,
             Alu.mult)
        r.op("vector", "tensor_scalar", self.isc[:, 3:4], self.isc[:, 0:1], SCL,
             None, Alu.mult)
        r.op("vector", "memset", self.rowp[0:1, 32:36], 0.0, inc=("vs", 1))
        st0.close()
        r.barrier()
        stp.close()
        r.barrier()

    def p1(self):
        r = self.r
        self.hb_es = contextlib.ExitStack()
        self.hb = r.sb("hb", [128, NT * T], BF16, self.hb_es)
        st1 = contextlib.ExitStack()
        xt = r.sb("xt1", [128, 2 * T], F32, st1)
        xb = r.sb("xb1", [128, 2 * T], BF16, st1)
        sq = r.sb("sq1", [128, 2 * T], BF16, st1)
        ps1 = r.psum("ps11", [128, T], F32, st1)
        ps2 = r.psum("ps21", [128, T], F32, st1)
        self.ln_stats(self.x_in, ps1, ps2, xt, xb, sq)
        self.ln_finalize(ps1, ps2, 1e-5, EPSF)
        self.bcast_row(self.statr, T, self.RB, ps1)
        self.bcast_row(self.statr, 0, self.mRB, ps1)
        cons = [r.val["vs"]] * 2
        for kt in range(NT):
            b = (kt % 2) * T
            r.wait(SY, "vs", cons[kt % 2])
            ldx = r.dma(SY, xt[:, b:b + T], self.x_in.ap()[kt * 128:(kt + 1) * 128, :],
                        "ld")
            r.wait("vector", "ld", ldx)
            r.op("vector", "tensor_tensor", xt[:, b:b + T], xt[:, b:b + T],
                 self.RB[:], Alu.mult)
            r.op("vector", "tensor_tensor", xt[:, b:b + T], xt[:, b:b + T],
                 self.mRB[:], Alu.subtract)
            cons[kt % 2] = r.op("vector", "tensor_scalar",
                                self.hb[:, kt * T:(kt + 1) * T], xt[:, b:b + T],
                                self.nvs[:, kt:kt + 1], None, Alu.mult,
                                inc=("vs", 1))
        st1.close()
        r.barrier()

    def p3(self):
        r = self.r
        st3 = contextlib.ExitStack()
        wqh = r.sb("wqh", [128, 6144], F32, st3)
        terq = r.sb("terq", [128, 3 * T], BF16, st3)
        sg1 = r.sb("sg1", [128, T], BF16, st3)
        sg2 = r.sb("sg2", [128, T], BF16, st3)
        qh = r.sb("qh", [128, T], BF16, st3)
        kh = r.sb("kh", [128, T], BF16, st3)
        vh = r.sb("vh", [128, T], BF16, st3)
        obuf = r.sb("obuf", [128, T], F32, st3)
        osq = r.sb("osq", [128, T], BF16, st3)
        maskm = r.sb("maskm", [128, 128], F32, st3)
        lamq = r.sb("lamq", [128, 128], F32, st3)
        ltmp = r.sb("ltmp", [128, 1], F32, st3)
        lrev = r.sb("lrev", [128, 1], F32, st3)
        lTc = r.sb("lTc", [128, 1], F32, st3)
        Sst = r.sb("Sst", [128, 128], F32, st3)
        Sb = r.sb("Sb", [128, 128], BF16, st3)
        kTs = r.sb("kTs", [128, 128], BF16, st3)
        vTs = r.sb("vTs", [128, 128], BF16, st3)
        scb = r.sb("scb", [128, 128], BF16, st3)
        qlam = r.sb("qlam", [128, 128], BF16, st3)
        psA = r.psum("psA", [128, T], F32, st3)
        psB = r.psum("psB", [128, 1536], F32, st3)
        psT = r.psum("psT", [128, 1024], BF16, st3)

        r.op("vector", "memset", self.ost[:], 0.0, inc=("vs", 1))
        q_done = r.val["vs"]
        st_done = r.val["ss"]
        for h in range(H):
            r.wait(SY, "vs", q_done)
            ldw = None
            for wi in range(3):
                col = wi * 2048 + h * 128
                for kt in range(NT):
                    ldw = r.dma(
                        SY,
                        wqh[:, wi * 2048 + kt * 128:wi * 2048 + (kt + 1) * 128],
                        bass.AP(self.qkvT, kt * 128 * 6144 + col,
                                [[6144, 128], [1, 128]]),
                        "ld")
            r.wait("vector", "ld", ldw)
            for wi in range(3):
                sl = slice(wi * 2048, (wi + 1) * 2048)
                vsv = r.op("vector", "tensor_scalar", wqh[:, sl], wqh[:, sl],
                           self.qsc[:, 0:1], None, Alu.mult, inc=("vs", 1))
                r.wait("scalar", "vs", vsv)
                r.op("scalar", "activation", sg1[:], wqh[:, sl], Act.Sign,
                     bias=self.csts[:, C_MH:C_MH + 1])
                ssv = r.op("scalar", "activation", sg2[:], wqh[:, sl], Act.Sign,
                           bias=self.csts[:, C_PH:C_PH + 1], inc=("ss", 1))
                r.wait("vector", "ss", ssv)
                q_done = r.op("vector", "tensor_tensor", terq[:, sl], sg1[:],
                              sg2[:], Alu.add, inc=("vs", 1))
            sc_h = self.scol[:, h:h + 1]
            r.op("vector", "tensor_scalar", maskm[:], self.diffs[:], sc_h, None,
                 Alu.mult)
            r.op("vector", "tensor_scalar", ltmp[:], self.iotacs[:], -1.0, 128.0,
                 Alu.mult, Alu.add)
            r.op("vector", "tensor_scalar", ltmp[:], ltmp[:], sc_h, None, Alu.mult)
            r.op("vector", "tensor_scalar", lTc[:], sc_h, -128.0, None, Alu.mult,
                 inc=("vs", 1))
            r.wait("scalar", "vs")
            r.op("scalar", "activation", maskm[:], maskm[:], Act.Exp, scale=-1.0)
            r.op("scalar", "activation", lrev[:], ltmp[:], Act.Exp, scale=-1.0)
            r.op("scalar", "activation", lTc[:], lTc[:], Act.Exp)
            ssv = r.op("scalar", "activation",
                       bass.AP(self.rowp, 0, [[T, 1], [1, 128]]), self.niotas[:],
                       Act.Exp, scale=self.svrow[0:1, h:h + 1], inc=("ss", 1))
            r.wait("vector", "ss", ssv)
            r.op("vector", "tensor_tensor", maskm[:], maskm[:], self.tris[:],
                 Alu.mult, inc=("vs", 1))
            r.wait("tensor", "vs")
            r.op("tensor", "matmul", psA[:, 1536:1664], self.onesMs[:],
                 self.rowp[:, 0:128], start=True, stop=True)
            tsv = r.inc_last("tensor", "ts")
            r.wait("vector", "ts", tsv)
            r.op("vector", "tensor_copy", lamq[:], psA[:, 1536:1664], inc=("vs", 1))
            r.wait("tensor", "vs")
            r.wait("tensor", "ss", st_done)
            for tc in range(4):
                for wi in range(3):
                    psl = slice(wi * 512, wi * 512 + 512)
                    for kt in range(NT):
                        r.op("tensor", "matmul", psA[:, psl],
                             terq[:, wi * T + kt * 128:wi * T + kt * 128 + 128],
                             self.hb[:, kt * T + tc * 512:kt * T + tc * 512 + 512],
                             start=(kt == 0), stop=(kt == NT - 1))
                tsv = r.inc_last("tensor", "ts")
                r.wait("scalar", "ts", tsv)
                tsl = slice(tc * 512, (tc + 1) * 512)
                r.op("scalar", "activation", qh[:, tsl], psA[:, 0:512], Act.Copy,
                     scale=self.isc[:, 3:4])
                r.op("scalar", "activation", kh[:, tsl], psA[:, 512:1024],
                     Act.Copy, scale=self.isc[:, 0:1])
                st_done = r.op("scalar", "activation", vh[:, tsl],
                               psA[:, 1024:1536], Act.Copy, scale=self.isc[:, 0:1],
                               inc=("ss", 1))
                r.wait("tensor", "ss", st_done)
            r.op("vector", "memset", Sst[:], 0.0)
            r.op("vector", "memset", Sb[:], 0.0, inc=("vs", 1))
            r.wait("tensor", "vs")
            for ci in range(16):
                cs = slice(ci * 128, (ci + 1) * 128)
                r.op("tensor", "matmul", psB[:, 0:128], kh[:, cs], qh[:, cs],
                     start=True, stop=True)
                r.op("tensor", "transpose", psT[:, 0:128], kh[:, cs], self.idbs[:])
                tsv = r.inc_last("tensor", "ts")
                r.wait("vector", "ts", tsv)
                r.op("vector", "tensor_tensor", scb[:], psB[:, 0:128], maskm[:],
                     Alu.mult)
                r.op("vector", "tensor_tensor", qlam[:], qh[:, cs], lamq[:],
                     Alu.mult, inc=("vs", 1))
                r.wait("scalar", "ts", tsv)
                ssv = r.op("scalar", "activation", kTs[:], psT[:, 0:128], Act.Copy,
                           scale=lrev[:, 0:1], inc=("ss", 1))
                r.wait("tensor", "ss", ssv)
                r.op("tensor", "transpose", psT[:, 512:640], vh[:, cs],
                     self.idbs[:])
                tsv = r.inc_last("tensor", "ts")
                r.wait("scalar", "ts", tsv)
                ssv = r.op("scalar", "activation", vTs[:], psT[:, 512:640],
                           Act.Copy, inc=("ss", 1))
                r.wait("tensor", "ss", ssv)
                r.wait("tensor", "vs")
                r.op("tensor", "matmul", psB[:, 512:640], vTs[:], scb[:],
                     start=True, stop=False)
                r.op("tensor", "matmul", psB[:, 512:640], Sb[:], qlam[:],
                     start=False, stop=True)
                r.op("tensor", "matmul", psB[:, 1024:1152], kTs[:], vTs[:],
                     start=True, stop=True)
                tsv = r.inc_last("tensor", "ts")
                r.wait("scalar", "ts", tsv)
                r.op("scalar", "activation", obuf[:, cs], psB[:, 512:640],
                     Act.Copy, inc=("ss", 1))
                r.wait("vector", "ts", tsv)
                r.op("vector", "scalar_tensor_tensor", Sst[:], Sst[:], lTc[:, 0:1],
                     psB[:, 1024:1152], Alu.mult, Alu.add)
                r.op("vector", "tensor_copy", Sb[:], Sst[:], inc=("vs", 1))
                r.wait("tensor", "vs")
            ssv = r.val["ss"]
            r.wait("scalar", "ss", ssv)
            st_done = r.op("scalar", "activation", osq[:], obuf[:], Act.Square,
                           inc=("ss", 1))
            r.wait("tensor", "ss", st_done)
            for tc in range(4):
                r.op("tensor", "matmul", psA[0:1, tc * 512:(tc + 1) * 512],
                     self.onescb[:], osq[:, tc * 512:(tc + 1) * 512],
                     start=True, stop=True)
            tsv = r.inc_last("tensor", "ts")
            r.wait("vector", "ts", tsv)
            vsv = r.op("vector", "tensor_tensor", self.ost[0:1, 0:T],
                       self.ost[0:1, 0:T], psA[0:1, :], Alu.add, inc=("vs", 1))
            r.wait("tensor", "vs", vsv)
            for tc in range(4):
                r.op("tensor", "matmul", psA[0:1, tc * 512:(tc + 1) * 512],
                     self.w1sqb[:, h:h + 1], osq[:, tc * 512:(tc + 1) * 512],
                     start=True, stop=True)
            tsv = r.inc_last("tensor", "ts")
            r.wait("vector", "ts", tsv)
            q_done = r.op("vector", "tensor_tensor", self.ost[0:1, T:2 * T],
                          self.ost[0:1, T:2 * T], psA[0:1, :], Alu.add,
                          inc=("vs", 1))
            r.wait(SY, "ss", st_done)
            r.dma(SY, self.o0.ap()[h * 128:(h + 1) * 128, :], obuf[:], "sd")
            if h == 0 and "dbg" in self.expose:
                for dst, srcb in ((self.hbs.ap()[0:128, 0:4], self.isc[:]),
                                  (self.hbs.ap()[0:128, 4:8], self.qsc[:]),
                                  (self.hbs.ap()[0:128, 8:24], self.scol[:]),
                                  (self.hbs.ap()[0:128, 24:152], maskm[:]),
                                  (self.hbs.ap()[0:128, 152:280], lamq[:]),
                                  (self.hbs.ap()[0:128, 280:281], lrev[:]),
                                  (self.hbs.ap()[0:128, 281:282], lTc[:])):
                    r.op("gpsimd", "dma_start", inc=("gl", 16), out=dst,
                         in_=srcb, allow_slow_non_contiguous=True)
                r.dma("gpsimd", self.hbs.ap()[128:256, :], qh[:], "gl")
                r.dma("gpsimd", self.hbs.ap()[256:384, :], kh[:], "gl")
                r.dma("gpsimd", self.hbs.ap()[384:512, :], vh[:], "gl")
                r.dma("gpsimd", self.hbs.ap()[512:640, :], terq[:, 0:T], "gl")
                r.dma("gpsimd", self.hbs.ap()[640:768, :], obuf[:], "gl")
                r.barrier()
        st3.close()
        r.barrier()

    def p3b(self):
        r = self.r
        self.hb_es.close()
        st = contextlib.ExitStack()
        y2b = r.sb("y2b", [128, NT * T], BF16, st)
        otb = r.sb("otb", [128, 2 * T], F32, st)
        R0B = r.sb("R0B", [128, T], F32, st)
        wsp = r.sb("wsp", [128, 4096], F32, st)
        trp = r.sb("trp", [128, 4096], BF16, st)
        sp1 = r.sb("sp1", [128, 4096], BF16, st)
        sp2 = r.sb("sp2", [128, 4096], BF16, st)
        xrt = r.sb("xrt", [128, T], F32, st)
        psP = r.psum("psP", [128, 2048], F32, st)
        stt = self.statr

        r.op("vector", "tensor_scalar", stt[0:1, 0:T], self.ost[0:1, 0:T], 1.0 / D,
             1e-6, Alu.mult, Alu.add)
        rp = self.rowp
        r.op("vector", "reciprocal", rp[0:1, 0:T], stt[0:1, 0:T], inc=("vs", 1))
        r.wait("scalar", "vs")
        r.op("scalar", "activation", stt[0:1, 0:T], rp[0:1, 0:T], Act.Sqrt,
             inc=("ss", 1))
        r.wait("vector", "ss")
        r.op("vector", "tensor_tensor", stt[0:1, T:2 * T], stt[0:1, 0:T],
             stt[0:1, 0:T], Alu.mult)
        r.op("vector", "tensor_copy", rp[0:1, 0:T], self.ost[0:1, T:2 * T])
        r.op("vector", "tensor_scalar", rp[0:1, 0:T], rp[0:1, 0:T], 1.0 / D,
             None, Alu.mult)
        r.op("vector", "tensor_tensor", stt[0:1, T:2 * T], stt[0:1, T:2 * T],
             rp[0:1, 0:T], Alu.mult)
        r.op("vector", "tensor_scalar", stt[0:1, T:2 * T], stt[0:1, T:2 * T], 1.0,
             EPSF, Alu.mult, Alu.add)
        r.op("vector", "reciprocal", rp[0:1, 0:T], stt[0:1, T:2 * T], inc=("vs", 1))
        r.wait("scalar", "vs")
        r.op("scalar", "activation", stt[0:1, T:2 * T], rp[0:1, 0:T], Act.Sqrt,
             inc=("ss", 1))
        r.wait("vector", "ss")
        r.op("vector", "tensor_tensor", stt[0:1, 0:T], stt[0:1, 0:T], stt[0:1, T:2 * T],
             Alu.mult, inc=("vs", 1))
        self.bcast_row(stt, 0, R0B, psP)
        cons = [r.val["vs"]] * 2
        for kt in range(NT):
            b = (kt % 2) * T
            r.wait(SY, "vs", cons[kt % 2])
            ldo = r.dma(SY, otb[:, b:b + T], self.o0.ap()[kt * 128:(kt + 1) * 128, :],
                        "ld")
            r.wait("vector", "ld", ldo)
            r.op("vector", "tensor_tensor", otb[:, b:b + T], otb[:, b:b + T],
                 R0B[:], Alu.mult)
            cons[kt % 2] = r.op("vector", "tensor_scalar",
                                y2b[:, kt * T:(kt + 1) * T], otb[:, b:b + T],
                                self.w12s[:, kt:kt + 1], None, Alu.mult,
                                inc=("vs", 1))
        self._quant_mm_evac(self.projT, self.qsc[:, 1:2], self.isc[:, 1:2],
                            y2b, self.x_in, self.xres1, wsp, trp, sp1, sp2, xrt,
                            psP, beta_tile=None)
        st.close()
        self.r.barrier()
        self.ost_es.close()

    def _quant_mm_evac(self, wT, rscale_col, iscale_col, rhs_b, res_in, res_out,
                       wsp, trp, sp1, sp2, xrt, psP, beta_tile):
        r = self.r
        wq_done = r.val["vs"]
        for mg in range(8):
            r.wait(SY, "vs", wq_done)
            ldw = None
            for kt in range(NT):
                ldw = r.dma(SY,
                            wsp[:, kt * 256:(kt + 1) * 256],
                            bass.AP(wT, kt * 128 * 2048 + mg * 256,
                                    [[2048, 128], [1, 256]]),
                            "ld")
            r.wait("vector", "ld", ldw)
            vsv = r.op("vector", "tensor_scalar", wsp[:], wsp[:], rscale_col, None,
                       Alu.mult, inc=("vs", 1))
            r.wait("scalar", "vs", vsv)
            r.op("scalar", "activation", sp1[:], wsp[:], Act.Sign,
                 bias=self.csts[:, C_MH:C_MH + 1])
            ssv = r.op("scalar", "activation", sp2[:], wsp[:], Act.Sign,
                       bias=self.csts[:, C_PH:C_PH + 1], inc=("ss", 1))
            r.wait("vector", "ss", ssv)
            vsv = r.op("vector", "tensor_tensor", trp[:], sp1[:], sp2[:], Alu.add,
                       inc=("vs", 1))
            r.wait("tensor", "vs", vsv)
            evd = {0: 0, 1: 0}
            for mt in range(2):
                r.wait(SY, "sd")
                ldr = r.dma(SY, xrt[:, 0:T],
                            res_in.ap()[(mg * 2 + mt) * 128:(mg * 2 + mt + 1) * 128, :],
                            "ld")
                r.wait("vector", "ld", ldr)
                for tc in range(4):
                    psl = slice(mt * 1024 + (tc % 2) * 512,
                                mt * 1024 + (tc % 2) * 512 + 512)
                    r.wait("tensor", "vs", evd[tc % 2])
                    for kt in range(NT):
                        r.op("tensor", "matmul", psP[:, psl],
                             trp[:, kt * 256 + mt * 128:kt * 256 + mt * 128 + 128],
                             rhs_b[:, kt * T + tc * 512:kt * T + tc * 512 + 512],
                             start=(kt == 0), stop=(kt == NT - 1))
                    tsv = r.inc_last("tensor", "ts")
                    r.wait("vector", "ts", tsv)
                    sl2 = slice(tc * 512, tc * 512 + 512)
                    if beta_tile is None:
                        wq_done = r.op("vector", "scalar_tensor_tensor",
                                       xrt[:, sl2], psP[:, psl], iscale_col,
                                       xrt[:, sl2], Alu.mult, Alu.add,
                                       inc=("vs", 1))
                        evd[tc % 2] = wq_done
                    else:
                        r.op("vector", "tensor_tensor", sp1[:, 0:512], psP[:, psl],
                             bass.AP(beta_tile, tc * 512, [[T, 128], [1, 512]]),
                             Alu.mult) if False else None
                        r.op("vector", "tensor_tensor",
                             bass.AP(wsp, 0, [[4096, 128], [1, 512]]),
                             psP[:, psl],
                             bass.AP(beta_tile, tc * 512, [[T, 128], [1, 512]]),
                             Alu.mult)
                        wq_done = r.op("vector", "tensor_tensor", xrt[:, sl2],
                                       xrt[:, sl2],
                                       bass.AP(wsp, 0, [[4096, 128], [1, 512]]),
                                       Alu.add, inc=("vs", 1))
                    evd[tc % 2] = wq_done
                r.wait(SY, "vs", wq_done)
                m = mg * 2 + mt
                r.dma(SY, res_out.ap()[m * 128:(m + 1) * 128, :], xrt[:, 0:T], "sd")

    def p4(self):
        r = self.r
        st4 = contextlib.ExitStack()
        xqb = r.sb("xqb4", [128, NT * T], BF16, st4)
        abB = r.sb("abB4", [128, T], F32, st4)
        st4t = contextlib.ExitStack()
        xt = r.sb("xt4", [128, 2 * T], F32, st4t)
        xn = r.sb("xn4", [128, T], F32, st4t)
        mx = r.sb("mx4", [128, T], F32, st4t)
        rm = r.sb("rm4", [128, 16], F32, st4t)
        zbl = r.sb("zbl4", [128, T], F32, st4t)
        btr = r.sb("btr4", [16, 128], F32, st4t)
        bbB = r.sb("bbB4", [128, T], F32, st4t)
        rbB = bbB
        xqi = r.sb("xqi4", [128, T], I16, st4t)
        gq = mx
        gq2 = zbl

        stS = contextlib.ExitStack()
        xb = r.sb("xb4", [128, 2 * T], BF16, stS)
        sq = r.sb("sq4", [128, 2 * T], BF16, stS)
        ps1 = r.psum("ps14", [128, T], F32, stS)
        ps2 = r.psum("ps24", [128, T], F32, stS)
        self.ln_stats(self.xres1, ps1, ps2, xt, xb, sq)
        self.ln_finalize(ps1, ps2, 1e-5, None)
        self.bcast_row(self.statr, T, self.RB, ps1)
        self.bcast_row(self.statr, 0, self.mRB, ps1)
        r.op("vector", "memset", mx[:], 0.0, inc=("vs", 1))
        cons = [r.val["vs"]] * 2
        for kt in range(NT):
            b = (kt % 2) * T
            r.wait(SY, "vs", cons[kt % 2])
            ldx = r.dma(SY, xt[:, b:b + T],
                        self.xres1.ap()[kt * 128:(kt + 1) * 128, :], "ld")
            r.wait("vector", "ld", ldx)
            r.op("vector", "tensor_tensor", xn[:], xt[:, b:b + T], self.RB[:],
                 Alu.mult)
            r.op("vector", "tensor_tensor", xn[:], xn[:], self.mRB[:],
                 Alu.subtract)
            r.op("vector", "tensor_tensor", mx[:], mx[:], xn[:], Alu.max)
            cons[kt % 2] = r.op("vector", "scalar_tensor_tensor", mx[:], xn[:],
                                -1.0, mx[:], Alu.mult, Alu.max, inc=("vs", 1))
        r.mark("p4beta")
        vsv = r.val["vs"]
        r.wait("tensor", "vs", vsv)
        for tc in range(16):
            r.op("tensor", "transpose", ps1[:, 0:128],
                 mx[:, tc * 128:(tc + 1) * 128], self.idfs[:])
            tsv = r.inc_last("tensor", "ts")
            r.wait("vector", "ts", tsv)
            vsv = r.op("vector", "reduce_max", rm[:, tc:tc + 1], ps1[:, 0:128], AX,
                       inc=("vs", 1))
            r.wait("tensor", "vs", vsv)
        r.op("tensor", "transpose", ps2[0:16, 0:128], rm[:], self.idfs[:])
        tsv = r.inc_last("tensor", "ts")
        r.wait("vector", "ts", tsv)
        r.op("vector", "tensor_copy", btr[:], ps2[0:16, 0:128])
        vsv = r.op("vector", "memset", zbl[:], 0.0, inc=("vs", 1))
        r.wait(SY, "vs", vsv)
        for tc in range(16):
            r.dma(SY, zbl[tc:tc + 1, tc * 128:(tc + 1) * 128],
                  btr[tc:tc + 1, 0:128], "ld")
        vsv = r.wait("vector", "ld")
        vsv = r.op("vector", "tensor_copy", btr[0:1, 0:1], btr[0:1, 0:1],
                   inc=("vs", 1))
        r.wait("tensor", "vs", vsv)
        for tc in range(4):
            sl = slice(tc * 512, (tc + 1) * 512)
            r.op("tensor", "matmul", ps1[:, sl], self.onesMs[:], zbl[:, sl],
                 start=True, stop=True)
        tsv = r.inc_last("tensor", "ts")
        r.wait("vector", "ts", tsv)
        r.op("vector", "tensor_scalar", bbB[:], ps1[:], 1.0 / 127.0, 1e-8,
             Alu.mult, Alu.max)
        r.op("vector", "tensor_scalar", abB[:], bbB[:], self.isc[:, 2:3], None,
             Alu.mult)
        r.op("vector", "reciprocal", rbB[:], bbB[:], inc=("vs", 1))
        cons = [r.val["vs"]] * 2
        for kt in range(NT):
            b = (kt % 2) * T
            r.wait(SY, "vs", cons[kt % 2])
            ldx = r.dma(SY, xt[:, b:b + T],
                        self.xres1.ap()[kt * 128:(kt + 1) * 128, :], "ld")
            r.wait("vector", "ld", ldx)
            r.op("vector", "tensor_tensor", xn[:], xt[:, b:b + T], self.RB[:],
                 Alu.mult)
            r.op("vector", "tensor_tensor", xn[:], xn[:], self.mRB[:],
                 Alu.subtract)
            r.op("vector", "tensor_tensor", xn[:], xn[:], rbB[:], Alu.mult)
            r.op("vector", "tensor_scalar", xqi[:], xn[:], 1.0, None, Alu.mult)
            r.op("vector", "scalar_tensor_tensor", gq[:], xqi[:], -1.0, xn[:],
                 Alu.mult, Alu.add)
            r.op("vector", "tensor_scalar", gq2[:], gq[:], -0.5, None, Alu.is_le)
            r.op("vector", "tensor_scalar", gq[:], gq[:], 0.5, None, Alu.is_ge)
            r.op("vector", "tensor_tensor", gq[:], gq[:], xqi[:], Alu.add)
            r.op("vector", "tensor_tensor", gq[:], gq[:], gq2[:], Alu.subtract)
            cons[kt % 2] = r.op("vector", "tensor_scalar",
                                xqb[:, kt * T:(kt + 1) * T], gq[:], -127.0, 127.0,
                                Alu.max, Alu.min, inc=("vs", 1))
        stS.close()
        r.barrier()
        st4t.close()
        r.barrier()
        st4b = contextlib.ExitStack()
        wsp = r.sb("wsp4", [128, 4096], F32, st4b)
        trp = r.sb("trp4", [128, 4096], BF16, st4b)
        sp1 = r.sb("sp14", [128, 4096], BF16, st4b)
        sp2 = r.sb("sp24", [128, 4096], BF16, st4b)
        xrt = r.sb("xrt4", [128, T], F32, st4b)
        psP = r.psum("psP4", [128, 2048], F32, st4b)
        self._quant_mm_evac(self.ffn0T, self.qsc[:, 2:3], None, xqb, self.xres1,
                            self.xres2, wsp, trp, sp1, sp2, xrt, psP,
                            beta_tile=abB)
        st4b.close()
        r.barrier()
        st4.close()

    def p5(self):
        r = self.r
        st5 = contextlib.ExitStack()
        xt = r.sb("xt5", [128, 2 * T], F32, st5)
        xlnb = r.sb("xlnb5", [128, NT * T], BF16, st5)

        stA = contextlib.ExitStack()
        xb = r.sb("xb5", [128, 2 * T], BF16, stA)
        sq = r.sb("sq5", [128, 2 * T], BF16, stA)
        xnf = r.sb("xnf5", [128, T], F32, stA)
        ps1 = r.psum("ps15", [128, T], F32, stA)
        ps2 = r.psum("ps25", [128, T], F32, stA)
        self.ln_stats(self.xres2, ps1, ps2, xt, xb, sq)
        self.ln_finalize(ps1, ps2, 1e-5, None)
        self.bcast_row(self.statr, T, self.RB, ps1)
        self.bcast_row(self.statr, 0, self.mRB, ps1)
        cons = [r.val["vs"]] * 2
        for kt in range(NT):
            b = (kt % 2) * T
            r.wait(SY, "vs", cons[kt % 2])
            ldx = r.dma(SY, xt[:, b:b + T],
                        self.xres2.ap()[kt * 128:(kt + 1) * 128, :], "ld")
            r.wait("vector", "ld", ldx)
            r.op("vector", "tensor_tensor", xnf[:], xt[:, b:b + T], self.RB[:],
                 Alu.mult)
            r.op("vector", "tensor_tensor", xnf[:], xnf[:], self.mRB[:],
                 Alu.subtract)
            r.op("vector", "tensor_copy", xlnb[:, kt * T:(kt + 1) * T], xnf[:])
            cons[kt % 2] = r.op("vector", "tensor_scalar", xt[:, b:b + T], xnf[:],
                                self.gsig1m[:, kt:kt + 1], None, Alu.mult,
                                inc=("vs", 1))
            r.wait(SY, "vs", cons[kt % 2])
            r.dma(SY, self.vg1s.ap()[kt * 128:(kt + 1) * 128, :], xt[:, b:b + T],
                  "sd")
        stA.close()
        r.barrier()
        stSH = contextlib.ExitStack()
        shb = r.sb("shb5", [128, NT * T], BF16, stSH)
        gl = None
        for kt in range(NT):
            gl = r.dma("gpsimd", shb[:, kt * T:(kt + 1) * T],
                       self.shiftT.ap()[kt * 128:(kt + 1) * 128, :], "gl")
        stB = contextlib.ExitStack()
        psSH = r.psum("psSH5", [128, 1024], F32, stB)
        r.wait("tensor", "gl", gl)
        ev_done = r.val["vs"]
        for m in range(NT):
            r.wait(SY, "vs", ev_done)
            mb = (m % 2) * T
            ldv = r.dma(SY, xt[:, mb:mb + T],
                        self.vg1s.ap()[m * 128:(m + 1) * 128, :], "ld")
            r.wait("vector", "ld", ldv)
            for tc in range(4):
                psl = slice((tc % 2) * 512, (tc % 2) * 512 + 512)
                src = (tc + 2) % 4
                for kt in range(NT):
                    r.op("tensor", "matmul", psSH[:, psl],
                         shb[:, kt * T + m * 128:kt * T + m * 128 + 128],
                         xlnb[:, kt * T + src * 512:kt * T + src * 512 + 512],
                         start=(kt == 0), stop=(kt == NT - 1))
                tsv = r.inc_last("tensor", "ts")
                r.wait("vector", "ts", tsv)
                sl2 = slice(mb + tc * 512, mb + tc * 512 + 512)
                ev_done = r.op("vector", "scalar_tensor_tensor", xt[:, sl2],
                               psSH[:, psl], self.gsig[:, m:m + 1], xt[:, sl2],
                               Alu.mult, Alu.add, inc=("vs", 1))
                r.wait("tensor", "vs", ev_done)
            r.wait(SY, "vs", ev_done)
            r.dma(SY, self.xshs.ap()[m * 128:(m + 1) * 128, :], xt[:, mb:mb + T],
                  "sd")
        stB.close()
        r.barrier()
        r.wait(SY, "sd")
        stSH.close()
        r.barrier()
        r.mark("p5ln2")
        stC = contextlib.ExitStack()
        xb = r.sb("xb5b", [128, 2 * T], BF16, stC)
        sq = r.sb("sq5b", [128, 2 * T], BF16, stC)
        xnf = r.sb("xnf5b", [128, T], F32, stC)
        ps1 = r.psum("ps15b", [128, T], F32, stC)
        ps2 = r.psum("ps25b", [128, T], F32, stC)
        self.ln_stats(self.xshs, ps1, ps2, xt, xb, sq)
        self.ln_finalize(ps1, ps2, 1e-5, None)
        self.bcast_row(self.statr, T, self.RB, ps1)
        self.bcast_row(self.statr, 0, self.mRB, ps1)
        xnb = xlnb
        cons = [r.val["vs"]] * 2
        for kt in range(NT):
            b = (kt % 2) * T
            r.wait(SY, "vs", cons[kt % 2])
            ldx = r.dma(SY, xt[:, b:b + T],
                        self.xshs.ap()[kt * 128:(kt + 1) * 128, :], "ld")
            r.wait("vector", "ld", ldx)
            r.op("vector", "tensor_tensor", xnf[:], xt[:, b:b + T], self.RB[:],
                 Alu.mult)
            r.op("vector", "tensor_tensor", xnf[:], xnf[:], self.mRB[:],
                 Alu.subtract)
            cons[kt % 2] = r.op("vector", "tensor_scalar",
                                xnb[:, kt * T:(kt + 1) * T], xnf[:],
                                self.nvs[:, 48 + kt:48 + kt + 1],
                                self.nvs[:, 64 + kt:64 + kt + 1], Alu.mult,
                                Alu.add, inc=("vs", 1))
        stC.close()
        r.barrier()
        r.mark("p5kv")
        stD = contextlib.ExitStack()
        wkb = r.sb("wkb5", [128, NT * 128], BF16, stD)
        kk = r.sb("kk5", [64, T], F32, stD)
        vv = r.sb("vv5", [64, T], F32, stD)
        wkv = r.sb("wkv5", [64, T], F32, stD)
        wrow = r.sb("wrow5", [64, T], F32, stD)
        states = r.sb("states5", [64, T], F32, stD)
        stb = r.sb("stb5", [64, T], BF16, stD)
        wob = r.sb("wob5", [64, NT * 128], BF16, stD)
        psKV = r.psum("psKV5", [128, 1024], F32, stD)
        gl = None
        for kt in range(NT):
            gl = r.dma("gpsimd", wkb[:, kt * 128:(kt + 1) * 128],
                       self.wkvT.ap()[kt * 128:(kt + 1) * 128, :], "gl")
        r.wait("tensor", "gl", gl)
        r.wait("tensor", "vs")
        for tc in range(4):
            sl = slice(tc * 512, (tc + 1) * 512)
            for kt in range(NT):
                r.op("tensor", "matmul", psKV[0:64, 0:512],
                     wkb[:, kt * 128:kt * 128 + 64],
                     xnb[:, kt * T + tc * 512:kt * T + tc * 512 + 512],
                     start=(kt == 0), stop=(kt == NT - 1))
                r.op("tensor", "matmul", psKV[0:64, 512:1024],
                     wkb[:, kt * 128 + 64:kt * 128 + 128],
                     xnb[:, kt * T + tc * 512:kt * T + tc * 512 + 512],
                     start=(kt == 0), stop=(kt == NT - 1))
            tsv = r.inc_last("tensor", "ts")
            r.wait("scalar", "ts", tsv)
            r.op("scalar", "activation", kk[:, sl], psKV[0:64, 0:512], Act.Copy)
            ssv = r.op("scalar", "activation", vv[:, sl], psKV[0:64, 512:1024],
                       Act.Copy, inc=("ss", 1))
            r.wait("tensor", "ss", ssv)
        r.wait("scalar", "ss")
        ssv = r.op("scalar", "activation", wkv[:], kk[:], Act.Sigmoid,
                   inc=("ss", 1))
        r.wait("vector", "ss", ssv)
        r.op("vector", "tensor_scalar", wkv[:], wkv[:], self.ucolv[:, 0:1], None,
             Alu.mult, inc=("vs", 1))
        r.wait("scalar", "vs")
        ssv = r.op("scalar", "activation", wkv[:], wkv[:], Act.Exp, scale=-1.0,
                   inc=("ss", 1))
        r.wait("vector", "ss", ssv)
        r.op("vector", "tensor_tensor", wkv[:], wkv[:], vv[:], Alu.mult)
        r.op("vector", "memset", wrow[:], 1.0)
        r.op("vector", "tensor_scalar", wrow[:], wrow[:], self.wcolv[:, 0:1],
             None, Alu.mult)
        r.op("vector", "tensor_tensor_scan", states[:, 0:1024], wrow[:, 0:1024],
             wkv[:, 0:1024], 0.0, Alu.mult, Alu.add)
        r.op("vector", "drain")
        r.op("vector", "tensor_tensor_scan", states[:, 1024:2048],
             wrow[:, 1024:2048], wkv[:, 1024:2048], states[:, 1023:1024],
             Alu.mult, Alu.add)
        r.op("vector", "tensor_copy", stb[:], states[:], inc=("vs", 1))
        if "dbg5" in self.expose:
            vsd = r.val["vs"]
            r.wait("gpsimd", "vs", vsd)
            r.dma("gpsimd", self.hbs.ap()[0:64, :], kk[:], "gl")
            r.dma("gpsimd", self.hbs.ap()[64:128, :], vv[:], "gl")
            r.dma("gpsimd", self.hbs.ap()[128:192, :], wkv[:], "gl")
            r.dma("gpsimd", self.hbs.ap()[192:256, :], states[:], "gl")
            r.dma("gpsimd", self.hbs.ap()[256:320, :], stb[:], "gl")
            r.wait(SY, "gl")
            r.barrier()
        gl = None
        for kt in range(NT):
            gl = r.dma("gpsimd", wob[:, kt * 128:(kt + 1) * 128],
                       self.woutT.ap()[0:64, kt * 128:(kt + 1) * 128], "gl")
        r.wait("tensor", "gl", gl)
        r.wait("tensor", "vs")
        ev_done = r.val["vs"]
        for mt in range(NT):
            r.wait(SY, "vs", ev_done)
            ldr = r.dma(SY, xt[:, 0:T], self.xres2.ap()[mt * 128:(mt + 1) * 128, :],
                        "ld")
            r.wait("vector", "ld", ldr)
            for tc in range(4):
                r.op("tensor", "matmul", psKV[:, 0:512],
                     wob[:, mt * 128:(mt + 1) * 128],
                     stb[:, tc * 512:(tc + 1) * 512], start=True, stop=True)
                tsv = r.inc_last("tensor", "ts")
                r.wait("vector", "ts", tsv)
                ev_done = r.op("vector", "tensor_tensor",
                               xt[:, tc * 512:(tc + 1) * 512],
                               xt[:, tc * 512:(tc + 1) * 512], psKV[:, 0:512],
                               Alu.add, inc=("vs", 1))
                r.wait("tensor", "vs", ev_done)
            r.wait(SY, "vs", ev_done)
            r.dma(SY, self.xres3.ap()[mt * 128:(mt + 1) * 128, :], xt[:, 0:T], "sd")
        stD.close()
        r.barrier()
        st5.close()

    def p6(self):
        r = self.r
        stq = contextlib.ExitStack()
        SB = 2048
        pi = r.sb("pi6", [128, 2 * SB], U8, stq)
        scd = r.sb("scd6", [128, 2 * (SB // 16)], F32, stq)
        aL = r.sb("aL6", [128, 2 * 2 * SB], U8, stq)
        sL = r.sb("sL6", [128, 2 * 2 * SB], U8, stq)
        taL = r.sb("taL6", [128, 2 * 2 * SB], BF16, stq)
        r1L = r.sb("r1L6", [128, 2 * 2 * SB], BF16, stq)
        r2L = r.sb("r2L6", [128, 2 * 2 * SB], BF16, stq)
        m2L = r.sb("m2L6", [128, 2 * SB], BF16, stq)
        sfL = r.sb("sfL6", [128, 2 * 2 * SB], BF16, stq)
        dec = r.sb("dec6", [128, 2 * 2 * SB], BF16, stq)

        def dequant(src_u8, src_sc, dst, nrows, nbytes):
            ntile = nrows // 128
            nsub = max(1, nbytes // SB)
            sb = min(nbytes, SB)
            nblk = sb // 16
            pcons = [r.val["vs"]] * 2
            wcons = [r.val["vs"]] * 2
            acons = [r.val["ss"]] * 2
            stq_q = (("scalar", "sd"), ("gpsimd", "sg"))
            dcons = [("sd", r.val["sd"])] * 2
            pend = None

            def front(it0):
                it, su = it0 // nsub, it0 % nsub
                sl = it0 % 2
                b = sl * SB
                bs = sl * (SB // 16)
                r.wait(SY, "vs", max(pcons[sl], wcons[sl]))
                r.dma(SY, pi[:, b:b + sb],
                      bass.AP(src_u8, it * 128 * nbytes + su * sb,
                              [[nbytes, 128], [1, sb]]), "ld")
                ldc = r.dma(SY, scd[:, bs:bs + nblk],
                            bass.AP(src_sc, it * 128 * (nbytes // 16) + su * nblk,
                                    [[nbytes // 16, 128], [1, nblk]]), "ld")
                sb2 = sl * 2 * SB
                r.wait("vector", "ld", ldc)
                r.wait("vector", "ss", acons[sl])
                r.op("vector", "tensor_scalar", aL[:, sb2:sb2 + sb],
                     pi[:, b:b + sb], 7, None, Alu.bitwise_and)
                r.op("vector", "tensor_scalar", aL[:, sb2 + sb:sb2 + 2 * sb],
                     pi[:, b:b + sb], 4, 7, Alu.logical_shift_right,
                     Alu.bitwise_and)
                r.op("vector", "tensor_scalar", sL[:, sb2:sb2 + sb],
                     pi[:, b:b + sb], 3, 1, Alu.logical_shift_right,
                     Alu.bitwise_and)
                r.op("vector", "tensor_scalar", sL[:, sb2 + sb:sb2 + 2 * sb],
                     pi[:, b:b + sb], 7, None, Alu.logical_shift_right)
                vsx = r.op("vector", "tensor_scalar", sfL[:, sb2:sb2 + 2 * sb],
                           sL[:, sb2:sb2 + 2 * sb], -2.0, 1.0, Alu.mult,
                           Alu.add, inc=("vs", 1))
                pcons[sl] = vsx
                r.wait("scalar", "vs", vsx)
                r.op("scalar", "activation", taL[:, sb2:sb2 + 2 * sb],
                     aL[:, sb2:sb2 + 2 * sb], Act.Copy, scale=0.5)
                r.op("scalar", "activation", r1L[:, sb2:sb2 + 2 * sb],
                     aL[:, sb2:sb2 + 2 * sb], Act.Relu,
                     bias=self.csts[:, C_M2:C_M2 + 1], scale=0.5)
                acons[sl] = r.op("scalar", "activation",
                                 r2L[:, sb2:sb2 + 2 * sb],
                                 aL[:, sb2:sb2 + 2 * sb], Act.Relu,
                                 bias=self.csts[:, C_M6:C_M6 + 1], scale=1.0,
                                 inc=("ss", 1))
                return (sl, it, su, acons[sl])

            def back(pend_t):
                sl, it, su, ssv = pend_t
                b2 = sl * 2 * SB
                bs = sl * (SB // 16)
                bd = sl * 2 * SB
                r.wait("vector", "ss", ssv)
                r.op("vector", "tensor_tensor", m2L[:, 0:2 * sb],
                     taL[:, b2:b2 + 2 * sb], r1L[:, b2:b2 + 2 * sb], Alu.add)
                r.op("vector", "tensor_tensor", m2L[:, 0:2 * sb],
                     m2L[:, 0:2 * sb], r2L[:, b2:b2 + 2 * sb], Alu.add)
                r.op("vector", "tensor_tensor", m2L[:, 0:2 * sb],
                     m2L[:, 0:2 * sb], sfL[:, b2:b2 + 2 * sb], Alu.mult)
                r.wait("vector", dcons[sl][0], dcons[sl][1])
                for half in range(2):
                    vsw = r.op(
                        "vector", "tensor_tensor",
                        bass.AP(dec, bd + half * 64,
                                [[2 * 2 * SB, 128], [128, sb // 64], [1, 64]]),
                        bass.AP(m2L, half * sb, [[2 * SB, 128], [64, sb // 64], [1, 64]]),
                        bass.AP(scd, bs, [[2 * (SB // 16), 128], [1, nblk], [0, 16]]),
                        Alu.mult, inc=("vs", 1))
                wcons[sl] = vsw
                qeng, qsem = stq_q[(it * nsub + su) % 2]
                r.wait(qeng, "vs", vsw)
                dcons[sl] = (qsem, r.dma(
                    qeng, bass.AP(dst, it * 128 * 2 * nbytes + su * 2 * sb,
                                  [[2 * nbytes, 128], [1, 2 * sb]]),
                    dec[:, bd:bd + 2 * sb], qsem))

            for it0 in range(ntile * nsub):
                t = front(it0)
                if pend is not None:
                    back(pend)
                pend = t
            back(pend)

        r.mark("p6deq")
        dequant(self.gp_u8, self.gsc, self.wdg, MLP, 1024)
        dequant(self.up_u8, self.usc, self.wdu, MLP, 1024)
        dequant(self.dn_u8, self.dsc, self.wdd, D, 8192)
        stq.close()
        r.barrier()

        r.mark("p6b")
        st6 = contextlib.ExitStack()
        acc = r.sb("acc6", [128, NT * TH], F32, st6)
        xph = r.sb("xph6", [128, NT * TH], BF16, st6)

        stS = contextlib.ExitStack()
        xt = r.sb("xt6", [128, 2 * T], F32, stS)
        xb = r.sb("xb6", [128, TH], BF16, stS)
        sq6 = r.sb("sq6", [128, TH], BF16, stS)
        xnf = r.sb("xnf6", [128, TH], F32, stS)
        ps1 = r.psum("ps16", [128, TH], F32, stS)
        ps2 = r.psum("ps26", [128, TH], F32, stS)
        cons = [r.val["vs"]] * 2
        scons = [r.val["ss"]] * 2
        tprev = r.val["ts"]
        for kt in range(NT):
            b = (kt % 2) * T
            r.wait(SY, "vs", cons[kt % 2])
            r.wait(SY, "ss", scons[kt % 2])
            ldx = r.dma(SY, xt[:, b:b + T],
                        self.xres3.ap()[kt * 128:(kt + 1) * 128, :], "ld")
            r.wait("vector", "ld", ldx)
            asl = slice(kt * TH, (kt + 1) * TH)
            r.op("vector", "tensor_scalar", acc[:, asl], xt[:, b:b + TH],
                 self.pcols[:, 1:2], None, Alu.mult)
            r.op("vector", "scalar_tensor_tensor", acc[:, asl],
                 xt[:, b + TH:b + T], self.pcols[:, 0:1], acc[:, asl],
                 Alu.mult, Alu.add)
            r.wait("vector", "ts", tprev)
            vsv = r.op("vector", "tensor_copy", xb[:], acc[:, asl], inc=("vs", 1))
            cons[kt % 2] = vsv
            r.wait("scalar", "vs", vsv)
            r.wait("scalar", "ts", tprev)
            ssv = r.op("scalar", "activation", sq6[:], acc[:, asl], Act.Square,
                       inc=("ss", 1))
            scons[kt % 2] = ssv
            r.wait("tensor", "vs", vsv)
            r.wait("tensor", "ss", ssv)
            for tc in range(2):
                sl = slice(tc * 512, (tc + 1) * 512)
                r.op("tensor", "matmul", ps1[0:1, sl], self.onescb[:], xb[:, sl],
                     start=(kt == 0), stop=(kt == NT - 1))
                r.op("tensor", "matmul", ps2[0:1, sl], self.onescb[:], sq6[:, sl],
                     start=(kt == 0), stop=(kt == NT - 1))
            tprev = r.inc_last("tensor", "ts")
        self.ln_finalize(ps1, ps2, 1e-5, None, tok=TH)
        self.bcast_row(self.statr, T, self.RB, ps1, tok=TH)
        self.bcast_row(self.statr, 0, self.mRB, ps1, tok=TH)
        for kt in range(NT):
            asl = slice(kt * TH, (kt + 1) * TH)
            r.op("vector", "tensor_tensor", xnf[:], acc[:, asl], self.RB[:, 0:TH],
                 Alu.mult)
            r.op("vector", "tensor_tensor", xnf[:], xnf[:], self.mRB[:, 0:TH],
                 Alu.subtract)
            vsv = r.op("vector", "tensor_copy", xb[:], xnf[:], inc=("vs", 1))
            r.wait("tensor", "vs", vsv)
            for tc in range(2):
                sl = slice(tc * 512, (tc + 1) * 512)
                r.op("tensor", "matmul", ps1[:, sl], self.perms[:], xb[:, sl],
                     start=True, stop=True)
            tsv = r.inc_last("tensor", "ts")
            r.wait("vector", "ts", tsv)
            r.op("vector", "tensor_copy", xph[:, asl], ps1[:], inc=("vs", 1))
        stS.close()
        r.barrier()

        r.mark("p6c")
        stC = contextlib.ExitStack()
        wst = r.sb("wst6", [128, 2 * 8192], BF16, stC)
        gbuf = r.sb("gbuf6", [128, 4 * TH], BF16, stC)
        itb = r.sb("itb6", [128, 8 * TH], BF16, stC)
        psG = r.psum("psG6", [128, 2048], F32, stC)
        psD = r.psum("psD6", [128, 2048], F32, stC)

        r.wait(SY, "sd")
        r.wait(SY, "sg")
        slotcons = [r.val["ts"]] * 2
        it_done = r.val["vs"]
        dn_done = r.val["vs"]
        gss = r.val["ss"]
        si = 0

        def mm_stage(mat, nc5):
            nonlocal si, it_done, gss
            s = si % 2
            sb_ = s * 8192
            si += 1
            src = self.wdg if mat == "g" else self.wdu
            r.wait(SY, "ts", slotcons[s])
            ldv = None
            for kt in range(NT):
                ldv = r.dma(SY, wst[:, sb_ + kt * 512:sb_ + kt * 512 + 512],
                            src.ap()[nc5 * 512:(nc5 + 1) * 512,
                                     kt * 128:(kt + 1) * 128],
                            "ld", transpose=True)
            r.wait("tensor", "ld", ldv)
            r.wait("tensor", "ss", gss)
            r.wait("tensor", "vs", it_done)
            if mat == "u":
                r.wait("vector", "ss", gss)
            else:
                r.wait("scalar", "vs", it_done)
            init = gss if mat == "g" else it_done
            evg = {0: init, 1: init}
            first = {0: True, 1: True}
            for nt in range(4):
                for tc in range(2):
                    g = (2 * nt + tc) % 2
                    gb = g * 1024
                    if not first[g]:
                        r.wait("tensor", "vs" if mat == "u" else "ss", evg[g])
                    first[g] = False
                    for kt in range(NT):
                        r.op("tensor", "matmul", psG[:, gb:gb + 512],
                             wst[:, sb_ + kt * 512 + nt * 128:
                                 sb_ + kt * 512 + nt * 128 + 128],
                             xph[:, kt * TH + tc * 512:kt * TH + tc * 512 + 512],
                             start=(kt == 0), stop=(kt == NT - 1))
                    tsv = r.inc_last("tensor", "ts")
                    if mat == "g":
                        r.wait("scalar", "ts", tsv)
                        gss = r.op("scalar", "activation",
                                   gbuf[:, nt * TH + tc * 512:nt * TH + tc * 512 + 512],
                                   psG[:, gb:gb + 512], Act.Gelu_apprx_tanh,
                                   inc=("ss", 1))
                        evg[g] = gss
                    else:
                        r.wait("vector", "ts", tsv)
                        it_done = r.op(
                            "vector", "tensor_tensor",
                            itb[:, ((nc5 % 2) * 4 + nt) * TH + tc * 512:
                                ((nc5 % 2) * 4 + nt) * TH + tc * 512 + 512],
                            gbuf[:, nt * TH + tc * 512:nt * TH + tc * 512 + 512],
                            psG[:, gb:gb + 512], Alu.mult, inc=("vs", 1))
                        evg[g] = it_done
            slotcons[s] = r.val["ts"]

        for nc5 in range(32):
            mm_stage("g", nc5)
            mm_stage("u", nc5)
            if nc5 % 2 == 1:
                r.wait(SY, "ts", slotcons[0])
                r.wait(SY, "ts", slotcons[1])
                si += si % 2
                ldd = None
                for j in range(8):
                    nglob = (nc5 - 1) * 512 + j * 128
                    ldd = r.dma(SY, wst[:, j * 2048:(j + 1) * 2048],
                                self.wdd.ap()[0:D, nglob:nglob + 128],
                                "ld", transpose=True)
                r.wait("tensor", "ld", ldd)
                r.wait("tensor", "vs", it_done)
                evd = {0: dn_done, 1: dn_done}
                dfirst = {0: True, 1: True}
                for mt in range(NT):
                    g = mt % 2
                    db = g * 1024
                    if not dfirst[g]:
                        r.wait("tensor", "vs", evd[g])
                    dfirst[g] = False
                    for tc in range(2):
                        for j in range(8):
                            r.op("tensor", "matmul",
                                 psD[:, db + tc * 512:db + (tc + 1) * 512],
                                 wst[:, j * 2048 + mt * 128:j * 2048 + mt * 128 + 128],
                                 itb[:, j * TH + tc * 512:j * TH + tc * 512 + 512],
                                 start=(j == 0), stop=(j == 7))
                    tsv = r.inc_last("tensor", "ts")
                    r.wait("vector", "ts", tsv)
                    dn_done = r.op("vector", "tensor_tensor",
                                   acc[:, mt * TH:(mt + 1) * TH],
                                   acc[:, mt * TH:(mt + 1) * TH],
                                   psD[:, db:db + 1024],
                                   Alu.add, inc=("vs", 1))
                    evd[g] = dn_done
                slotcons[0] = r.val["ts"]
                slotcons[1] = r.val["ts"]
        for mt in range(NT):
            r.wait(SY, "vs", dn_done)
            r.dma(SY, self.out_t.ap()[mt * 128:(mt + 1) * 128, :],
                  acc[:, mt * TH:(mt + 1) * TH], "so")
        r.wait(SY, "so")
        stC.close()
        r.barrier()
        st6.close()


def build_program(upto=9, expose=()):
    kb = KB(expose=expose)
    kb.r.mark("p0")
    kb.p0()
    if upto >= 1:
        kb.r.barrier()
        kb.r.mark("p1")
        kb.p1()
        if "hbs" in expose:
            for kt in range(NT):
                kb.r.dma("gpsimd", kb.hbs.ap()[kt * 128:(kt + 1) * 128, :],
                         kb.hb[:, kt * T:(kt + 1) * T], "gl")
            kb.r.barrier()
    if upto >= 2:
        kb.r.barrier()
        kb.r.mark("p3")
        kb.p3()
    if upto >= 3:
        kb.r.barrier()
        kb.r.mark("p3b")
        kb.p3b()
    if upto >= 4:
        kb.r.barrier()
        kb.r.mark("p4")
        kb.p4()
    if upto >= 5:
        kb.r.barrier()
        kb.r.mark("p5")
        kb.p5()
    if upto >= 6:
        kb.r.barrier()
        kb.r.mark("p6")
        kb.p6()
    else:
        kb.epilogue_zero()
    return kb.r.build()


def _host_inputs(inputs):
    f32 = np.float32
    x = np.asarray(inputs["L0_attn_x"], f32)
    qkv_w = np.asarray(inputs["L0_attn_qkv_w"], f32)
    proj_w = np.asarray(inputs["L0_attn_proj_w"], f32)
    ffn0_w = np.asarray(inputs["L0_ffn_weight"], f32)
    wk = np.asarray(inputs["L1_attn_W_key"], f32)
    wv = np.asarray(inputs["L1_attn_W_value"], f32)
    wo = np.asarray(inputs["L1_attn_W_output"], f32)
    wsh = np.asarray(inputs["L1_attn_W_shift"], f32)
    gate = np.asarray(inputs["L1_attn_gate"], f32)
    tdec = np.asarray(inputs["L1_attn_time_decay"], f32)
    tfir = np.asarray(inputs["L1_attn_time_first"], f32)
    lng = np.asarray(inputs["L1_attn_ln_gamma"], f32)
    lnb = np.asarray(inputs["L1_attn_ln_beta"], f32)
    qkvnw = np.asarray(inputs["L0_attn_qkv_norm_w"], f32)
    projnw = np.asarray(inputs["L0_attn_proj_norm_w"], f32)
    normw = np.asarray(inputs["L0_attn_norm_w"], f32)
    svec = np.asarray(inputs["L0_attn_s"], f32)

    perm_idx = np.arange(MLP).reshape(-1, 128)
    perm_blk = np.concatenate([np.arange(0, 128, 2), np.arange(1, 128, 2)])
    perm_idx = (perm_idx[:, perm_blk] * 0 + perm_idx[:, :1] * 0 +
                (np.arange(MLP).reshape(-1, 128)[:, perm_blk])).reshape(-1)

    def u8view(a):
        a = np.ascontiguousarray(np.asarray(a, np.int32))
        return a.view(np.uint8)[:, ::4].copy()

    gp = u8view(inputs["L1_ffn_gate_W_packed"])[perm_idx]
    up = u8view(inputs["L1_ffn_up_W_packed"])[perm_idx]
    dn = u8view(inputs["L1_ffn_down_W_packed"])
    gs = np.ascontiguousarray(np.asarray(inputs["L1_ffn_gate_scales"], f32))[perm_idx]
    us = np.ascontiguousarray(np.asarray(inputs["L1_ffn_up_scales"], f32))[perm_idx]
    ds = np.ascontiguousarray(np.asarray(inputs["L1_ffn_down_scales"], f32))

    def tiles16(v):
        return np.ascontiguousarray(v.reshape(16, 128).T)

    nvec = np.concatenate([tiles16(qkvnw), tiles16(normw), tiles16(projnw),
                           tiles16(lng), tiles16(lnb)], axis=1)
    gvec = tiles16(gate)
    cstv = np.zeros((128, 8), f32)
    cstv[:, C_M2] = -2.0
    cstv[:, C_M6] = -6.0
    cstv[:, C_MH] = -0.5
    cstv[:, C_PH] = 0.5
    idnb = np.eye(128, dtype=ml_dtypes.bfloat16)
    idnf = np.eye(128, dtype=f32)
    onesM = np.ones((128, 128), f32)
    tt = np.arange(128, dtype=f32)
    diffT = tt[None, :] - tt[:, None]
    triT = (diffT >= 0).astype(f32)
    permM = np.zeros((128, 128), f32)
    sigma = np.concatenate([np.arange(0, 128, 2), np.arange(1, 128, 2)])
    permM[sigma, np.arange(128)] = 1.0
    permM = permM.astype(ml_dtypes.bfloat16)
    niotaR = (-tt)[None, :]
    iotaC = tt[:, None]

    common = dict(
        qkvT=np.ascontiguousarray(qkv_w.T), projT=np.ascontiguousarray(proj_w.T),
        ffn0T=np.ascontiguousarray(ffn0_w.T), shiftT=np.ascontiguousarray(wsh.T),
        wkvT=np.ascontiguousarray(np.concatenate([wk.T, wv.T], axis=1)),
        woutT=np.ascontiguousarray(wo.T),
        nvec=nvec, gvec=gvec, svec=svec[None, :],
        tdvec=np.ascontiguousarray(np.stack([tdec, tfir], axis=1)),
        gp_u8=gp, up_u8=up, dn_u8=dn, gsc=gs, usc=us, dsc=ds,
        cst=cstv, idnb=idnb, idnf=idnf, onesM=onesM, diffT=diffT, triT=triT,
        permM=permM, niotaR=niotaR, iotaC=iotaC,
    )
    in_maps = []
    for c in range(8):
        b, p = c // 2, c % 2
        m = dict(common)
        m["x_fm"] = np.ascontiguousarray(x[b].T)
        pc = np.zeros((128, 2), f32)
        pc[:, 0] = float(p)
        pc[:, 1] = 1.0 - float(p)
        m["pcol"] = pc
        in_maps.append(m)
    return in_maps


def kernel(**inputs):
    in_maps = _host_inputs(inputs)
    nc = build_program()
    res = run_bass_kernel_spmd(nc, in_maps, core_ids=list(range(8)))
    out = np.empty((4, T, D), np.float32)
    for c in range(8):
        b, p = c // 2, c % 2
        out[b, p * TH:(p + 1) * TH, :] = np.asarray(res.results[c]["out"]).T
    return out


# version 13 (final)
# speedup vs baseline: 1.0525x; 1.0525x over previous
import contextlib
import numpy as np
import ml_dtypes

import concourse.bass as bass
import concourse.mybir as mybir
from concourse.bass_utils import run_bass_kernel_spmd

F32 = mybir.dt.float32
BF16 = mybir.dt.bfloat16
I16 = mybir.dt.int16
U8 = mybir.dt.uint8
Alu = mybir.AluOpType
Act = mybir.ActivationFunctionType
AX = mybir.AxisListType.X

D = 2048
T = 2048
TH = 1024
NT = 16
H = 16
MLP = 16384
EPSF = float(np.finfo(np.float32).eps)
SCL = float(128.0 ** -0.5)

ENGS = ("sync", "vector", "scalar", "tensor", "gpsimd")
C_M2, C_M6, C_MH, C_PH = 0, 1, 2, 3
SY = "sync"


class Rec:
    def __init__(self, nc):
        self.nc = nc
        self.cmds = {e: [] for e in ENGS}
        self.es = contextlib.ExitStack()
        self.sems = {}
        self.val = {}
        self.wset = {e: {} for e in ENGS}
        self.rset = {e: set() for e in ENGS}

    def sem(self, name):
        if name not in self.sems:
            self.sems[name] = self.es.enter_context(self.nc.semaphore(name))
            self.val[name] = 0
        return self.sems[name]

    def op(self, eng, meth, *a, inc=None, **k):
        semtup = None
        if inc is not None:
            sname, n = inc
            semh = self.sem(sname)
            self.val[sname] += n
            semtup = (semh, n)
        if eng in ("vector", "scalar") and meth not in ("dma_start", "sem_inc"):
            if meth == "drain":
                self.wset[eng].clear()
                self.rset[eng].clear()
                self.cmds[eng].append(("op", meth, a, k, semtup))
                return self.val[inc[0]] if inc else None
            outn = self._apname(a[0]) if a else None
            outoff = getattr(a[0], "offset", None) if a else None
            fs = getattr(a[0], "free_size", None) if a else None
            if callable(fs):
                fs = fs()
            risky = (meth == "memset" or not isinstance(fs, int) or fs < 1024
                     or not isinstance(outoff, int))
            ins = []
            for x in list(a[1:]) + list(k.values()):
                n = self._apname(x)
                if n:
                    ins.append((n, getattr(x, "offset", None)))
            need = False
            for n, off in ins:
                w = self.wset[eng].get(n)
                if w is None:
                    continue
                woff, wrisky = w
                if wrisky or meth == "tensor_tensor_scan" \
                        or not isinstance(off, int) or off != woff:
                    need = True
                    break
            if need:
                self.cmds[eng].append(("op", "drain", (), {}, None))
                self.wset[eng].clear()
                self.rset[eng].clear()
            if outn:
                self.wset[eng][outn] = (outoff, risky)
            for n, _ in ins:
                self.rset[eng].add(n)
        self.cmds[eng].append(("op", meth, a, k, semtup))
        return self.val[inc[0]] if inc else None

    @staticmethod
    def _apname(x):
        n = getattr(x, "name", None)
        if callable(n):
            try:
                n = n()
            except Exception:
                return None
        return n if isinstance(n, str) else None

    def mark(self, name):
        for e in ENGS:
            self.cmds[e].append(("scope", name))

    def barrier(self, engs=ENGS):
        b = self.sem("bar")
        for e in engs:
            self.cmds[e].append(("op", "sem_inc", (b, 1), {}, None))
            self.val["bar"] += 1
        tgt = self.val["bar"]
        for e in engs:
            self.cmds[e].append(("wait", b, tgt))

    def inc_last(self, eng, sname, n=1):
        semh = self.sem(sname)
        self.val[sname] += n
        for i in range(len(self.cmds[eng]) - 1, -1, -1):
            c = self.cmds[eng][i]
            if c[0] == "op":
                assert c[4] is None, "last op already has an inc"
                self.cmds[eng][i] = (c[0], c[1], c[2], c[3], (semh, n))
                return self.val[sname]
        raise RuntimeError("no op to attach inc")

    def wait(self, eng, sname, value=None):
        semh = self.sem(sname)
        v = self.val[sname] if value is None else value
        self.cmds[eng].append(("wait", semh, v))
        return v

    def sb(self, name, shape, dtype, stack=None):
        return (stack or self.es).enter_context(self.nc.sbuf_tensor(name, shape, dtype))

    def psum(self, name, shape, dtype=F32, stack=None):
        return (stack or self.es).enter_context(self.nc.psum_tensor(name, shape, dtype))

    def dma(self, eng, out, in_, sname, transpose=False):
        if transpose:
            return self.op(eng, "dma_start", inc=(sname, 16), out=out, in_=in_,
                           transpose=True)
        return self.op(eng, "dma_start", inc=(sname, 16), out=out, in_=in_)

    def build(self):
        nc = self.nc
        with nc.Block() as block:
            def make(eng_name):
                def runner(eng):
                    cur = None
                    for c in self.cmds[eng_name]:
                        if c[0] == "op":
                            _, meth, a, k, semtup = c
                            ins = getattr(eng, meth)(*a, **k)
                            if semtup is not None:
                                ins.then_inc(*semtup)
                        elif c[0] == "scope":
                            if cur is not None:
                                nc.leave_named_scope(cur[0], cur[1], False)
                            sid, _ = nc.enter_named_scope(c[1], False)
                            cur = (c[1], sid)
                        else:
                            eng.wait_ge(c[1], c[2])
                    if cur is not None:
                        nc.leave_named_scope(cur[0], cur[1], False)
                return runner
            block.sync(make("sync"))
            block.vector(make("vector"))
            block.scalar(make("scalar"))
            block.tensor(make("tensor"))
            block.gpsimd(make("gpsimd"))
        return nc


class KB:

    def __init__(self, expose=()):
        self.nc = bass.Bass("TRN2", target_bir_lowering=False, debug=False,
                            detect_race_conditions=False)
        self.r = Rec(self.nc)
        self.expose = set(expose)
        self._declare_io()
        self._declare_shared()

    def din(self, name, shape, dtype):
        return self.nc.dram_tensor(name, shape, dtype, kind="ExternalInput")

    def dscr(self, name, shape, dtype):
        kind = "ExternalOutput" if name in self.expose else "Internal"
        return self.nc.dram_tensor(name, shape, dtype, kind=kind)

    def _declare_io(self):
        d = self.din
        self.x_in = d("x_fm", [D, T], F32)
        self.qkvT = d("qkvT", [D, 3 * D], F32)
        self.projT = d("projT", [D, D], F32)
        self.ffn0T = d("ffn0T", [D, D], F32)
        self.shiftT = d("shiftT", [D, D], F32)
        self.wkvT = d("wkvT", [D, 128], F32)
        self.woutT = d("woutT", [64, D], F32)
        self.nvec = d("nvec", [128, 80], F32)
        self.gvec = d("gvec", [128, 16], F32)
        self.svec = d("svec", [1, 16], F32)
        self.tdvec = d("tdvec", [64, 2], F32)
        self.gp_u8 = d("gp_u8", [MLP, 1024], U8)
        self.up_u8 = d("up_u8", [MLP, 1024], U8)
        self.dn_u8 = d("dn_u8", [D, 8192], U8)
        self.gsc = d("gsc", [MLP, 64], F32)
        self.usc = d("usc", [MLP, 64], F32)
        self.dsc = d("dsc", [D, 512], F32)
        self.cst = d("cst", [128, 8], F32)
        self.idnb = d("idnb", [128, 128], BF16)
        self.idnf = d("idnf", [128, 128], F32)
        self.onesM = d("onesM", [128, 128], F32)
        self.diffT = d("diffT", [128, 128], F32)
        self.triT = d("triT", [128, 128], F32)
        self.permM = d("permM", [128, 128], BF16)
        self.niotaR = d("niotaR", [1, 128], F32)
        self.iotaC = d("iotaC", [128, 1], F32)
        self.pcol = d("pcol", [128, 2], F32)
        self.out_t = self.nc.dram_tensor("out", [D, TH], F32, kind="ExternalOutput")
        s = self.dscr
        self.xres1 = s("xres1", [D, T], F32)
        self.xres2 = s("xres2", [D, T], F32)
        self.xres3 = s("xres3", [D, T], F32)
        self.o0 = s("o0", [D, T], F32)
        self.hbs = s("hbs", [D, T], F32)
        self.xshs = s("xshs", [D, T], F32)
        self.vg1s = s("vg1s", [D, T], F32)
        self.wdg = s("wdg", [MLP, D], BF16)
        self.wdu = s("wdu", [MLP, D], BF16)
        self.wdd = s("wdd", [D, MLP], BF16)

    def _declare_shared(self):
        r = self.r
        self.csts = r.sb("csts", [128, 8], F32)
        self.idbs = r.sb("idbs", [128, 128], BF16)
        self.idfs = r.sb("idfs", [128, 128], F32)
        self.onesMs = r.sb("onesMs", [128, 128], F32)
        self.diffs = r.sb("diffs", [128, 128], F32)
        self.tris = r.sb("tris", [128, 128], F32)
        self.perms = r.sb("perms", [128, 128], BF16)
        self.niotas = r.sb("niotas", [1, 128], F32)
        self.iotacs = r.sb("iotacs", [128, 1], F32)
        self.pcols = r.sb("pcols", [128, 2], F32)
        self.nvs = r.sb("nvs", [128, 80], F32)
        self.gsig = r.sb("gsig", [128, 16], F32)
        self.gsig1m = r.sb("gsig1m", [128, 16], F32)
        self.w12s = r.sb("w12s", [128, 16], F32)
        self.w1sqb = r.sb("w1sqb", [128, 16], BF16)
        self.stw16 = r.sb("stw16", [128, 32], BF16)
        self.scol = r.sb("scol", [128, 16], F32)
        self.svrow = r.sb("svrow", [1, 16], F32)
        self.tds = r.sb("tds", [64, 2], F32)
        self.wcolv = r.sb("wcolv", [64, 1], F32)
        self.ucolv = r.sb("ucolv", [64, 1], F32)
        self.onescb = r.sb("onescb", [128, 1], BF16)
        self.onescf = r.sb("onescf", [128, 1], F32)
        self.qsc = r.sb("qsc", [128, 4], F32)
        self.isc = r.sb("isc", [128, 4], F32)
        self.rowp = r.sb("rowp", [128, T], F32)
        self.statr = r.sb("statr", [1, 2 * T], F32)
        self.RB = r.sb("RB", [128, T], F32)
        self.mRB = r.sb("mRB", [128, T], F32)
        self.ost_es = contextlib.ExitStack()
        self.ost = r.sb("ost", [1, 2 * T], F32, self.ost_es)

    def bcast_row(self, src_t, src_off, dst, pst, tok=T):
        r = self.r
        vsv = r.op("vector", "tensor_copy", self.rowp[0:1, 0:tok],
                   bass.AP(src_t, src_off, [[2 * T, 1], [1, tok]]), inc=("vs", 1))
        r.wait("tensor", "vs", vsv)
        for tc in range(tok // 512):
            sl = slice(tc * 512, (tc + 1) * 512)
            r.op("tensor", "matmul", pst[:, sl], self.onesMs[:],
                 self.rowp[:, sl], start=True, stop=True)
        tsv = r.inc_last("tensor", "ts")
        r.wait("vector", "ts", tsv)
        r.op("vector", "tensor_copy", dst[:, 0:tok], pst[:, 0:tok], inc=("vs", 1))

    def ln_stats(self, src_t, ps1, ps2, xt, xbt, sqt):
        r = self.r
        cons = [r.val["vs"]] * 2
        scons = [r.val["ss"]] * 2
        tcons = [r.val["ts"]] * 2
        for kt in range(NT):
            b = (kt % 2) * T
            r.wait(SY, "vs", cons[kt % 2])
            r.wait(SY, "ss", scons[kt % 2])
            ldx = r.dma(SY, xt[:, b:b + T], src_t.ap()[kt * 128:(kt + 1) * 128, :],
                        "ld")
            r.wait("vector", "ld", ldx)
            r.wait("vector", "ts", tcons[kt % 2])
            vsv = r.op("vector", "tensor_copy", xbt[:, b:b + T], xt[:, b:b + T],
                       inc=("vs", 1))
            cons[kt % 2] = vsv
            r.wait("scalar", "vs", vsv)
            r.wait("scalar", "ts", tcons[kt % 2])
            ssv = r.op("scalar", "activation", sqt[:, b:b + T], xt[:, b:b + T],
                       Act.Square, inc=("ss", 1))
            scons[kt % 2] = ssv
            r.wait("tensor", "vs", vsv)
            r.wait("tensor", "ss", ssv)
            for tc in range(4):
                sl = slice(tc * 512, (tc + 1) * 512)
                r.op("tensor", "matmul", ps1[0:1, sl], self.onescb[:],
                     xbt[:, b + tc * 512:b + (tc + 1) * 512],
                     start=(kt == 0), stop=(kt == NT - 1))
                r.op("tensor", "matmul", ps2[0:1, sl], self.onescb[:],
                     sqt[:, b + tc * 512:b + (tc + 1) * 512],
                     start=(kt == 0), stop=(kt == NT - 1))
            tcons[kt % 2] = r.inc_last("tensor", "ts")

    def ln_finalize(self, ps1, ps2, eps_ln, rms_eps, tok=T):
        r = self.r
        st = self.statr
        r.wait("vector", "ts")
        r.op("vector", "tensor_scalar", st[0:1, 0:tok], ps1[0:1, 0:tok], 1.0 / D,
             None, Alu.mult)
        r.op("vector", "tensor_scalar", st[0:1, T:T + tok], ps2[0:1, 0:tok], 1.0 / D,
             None, Alu.mult)
        r.op("vector", "tensor_tensor", self.rowp[0:1, 0:tok], st[0:1, 0:tok],
             st[0:1, 0:tok], Alu.mult)
        r.op("vector", "tensor_tensor", st[0:1, T:T + tok], st[0:1, T:T + tok],
             self.rowp[0:1, 0:tok], Alu.subtract)
        if rms_eps is None:
            r.op("vector", "tensor_scalar", st[0:1, T:T + tok], st[0:1, T:T + tok], 1.0,
                 eps_ln, Alu.mult, Alu.add)
        else:
            r.op("vector", "tensor_scalar", st[0:1, T:T + tok], st[0:1, T:T + tok],
                 1.0 + rms_eps, rms_eps * eps_ln, Alu.mult, Alu.add)
        r.op("vector", "reciprocal", self.rowp[0:1, 0:tok], st[0:1, T:T + tok], inc=("vs", 1))
        r.wait("scalar", "vs")
        r.op("scalar", "activation", st[0:1, T:T + tok], self.rowp[0:1, 0:tok], Act.Sqrt,
             inc=("ss", 1))
        r.wait("vector", "ss")
        r.op("vector", "tensor_tensor", st[0:1, 0:tok], st[0:1, 0:tok],
             st[0:1, T:T + tok], Alu.mult, inc=("vs", 1))

    def epilogue_zero(self):
        r = self.r
        stf = contextlib.ExitStack()
        z = r.sb("zfin", [128, TH], F32, stf)
        r.op("vector", "memset", z[:], 0.0, inc=("vs", 1))
        r.wait(SY, "vs")
        for mt in range(NT):
            r.dma(SY, self.out_t.ap()[mt * 128:(mt + 1) * 128, :], z[:], "so")
        r.wait(SY, "so")
        stf.close()

    def p0(self):
        r = self.r
        for dst, src in ((self.csts, self.cst), (self.idbs, self.idnb),
                         (self.idfs, self.idnf), (self.onesMs, self.onesM),
                         (self.diffs, self.diffT), (self.tris, self.triT),
                         (self.perms, self.permM), (self.niotas, self.niotaR),
                         (self.iotacs, self.iotaC), (self.pcols, self.pcol),
                         (self.nvs, self.nvec), (self.gsig, self.gvec),
                         (self.svrow, self.svec), (self.tds, self.tdvec)):
            r.dma(SY, dst[:], src.ap(), "ld")
        ld0 = r.val["ld"]
        for e in ("vector", "scalar", "tensor"):
            r.wait(e, "ld", ld0)
        r.op("vector", "memset", self.onescb[:], 1.0)
        r.op("vector", "memset", self.onescf[:], 1.0)
        r.op("vector", "memset", self.rowp[:], 0.0)
        r.op("vector", "memset", self.stw16[:], 1.0)
        r.op("vector", "tensor_tensor", self.w12s[:], self.nvs[:, 16:32],
             self.nvs[:, 32:48], Alu.mult, inc=("vs", 1))
        r.wait("scalar", "vs")
        r.op("scalar", "activation", self.w1sqb[:], self.nvs[:, 16:32], Act.Square)
        r.op("scalar", "activation", self.gsig[:], self.gsig[:], Act.Sigmoid)
        r.op("scalar", "activation", self.wcolv[:], self.tds[:, 0:1], Act.Exp)
        r.op("scalar", "activation", self.ucolv[:], self.tds[:, 1:2], Act.Exp,
             inc=("ss", 1))
        r.wait("vector", "ss")
        r.op("vector", "tensor_scalar", self.gsig1m[:], self.gsig[:], -1.0, 1.0,
             Alu.mult, Alu.add)
        r.op("vector", "tensor_copy", bass.AP(self.stw16, 1, [[32, 128], [2, 16]]),
             self.w1sqb[:])
        r.op("vector", "tensor_copy", self.rowp[0:1, 0:16], self.svrow[:],
             inc=("vs", 1))
        stp = contextlib.ExitStack()
        psP0 = r.psum("psP0", [128, 512], F32, stp)
        r.wait("tensor", "vs")
        r.op("tensor", "matmul", psP0[:, 0:16], self.onesMs[:], self.rowp[:, 0:16],
             start=True, stop=True)
        r.inc_last("tensor", "ts")
        r.wait("vector", "ts")
        r.op("vector", "tensor_copy", self.scol[:], psP0[:, 0:16])
        r.op("vector", "memset", self.rowp[0:1, 0:16], 0.0, inc=("vs", 1))

        st0 = contextlib.ExitStack()
        wst = r.sb("wst0", [128, 2 * 6144], F32, st0)
        acc4 = r.sb("acc40", [128, 4], F32, st0)
        red49 = r.sb("red490", [128, 49], F32, st0)
        r.op("vector", "memset", acc4[:], 0.0, inc=("vs", 1))
        cons = [r.val["vs"]] * 2
        for wi, (wt, wcols) in enumerate(((self.qkvT, 6144), (self.projT, 2048),
                                          (self.ffn0T, 2048))):
            for kt in range(NT):
                b = (kt % 2) * 6144
                r.wait(SY, "vs", cons[kt % 2])
                ldw = r.dma(SY, wst[:, b:b + wcols],
                            wt.ap()[kt * 128:(kt + 1) * 128, :], "ld")
                r.wait("vector", "ld", ldw)
                nch = wcols // 128
                r.op("vector", "tensor_reduce", red49[:, 0:nch],
                     bass.AP(wst, b, [[2 * 6144, 128], [128, nch], [1, 128]]),
                     AX, Alu.add, True)
                r.op("vector", "tensor_reduce", red49[:, 48:49], red49[:, 0:nch],
                     AX, Alu.add)
                cons[kt % 2] = r.op("vector", "tensor_tensor", acc4[:, wi:wi + 1],
                                    acc4[:, wi:wi + 1], red49[:, 48:49], Alu.add,
                                    inc=("vs", 1))
        r.wait("tensor", "vs")
        r.op("tensor", "matmul", psP0[0:1, 32:36], self.onescf[:], acc4[:],
             start=True, stop=True)
        r.inc_last("tensor", "ts")
        r.wait("vector", "ts")
        r.op("vector", "tensor_copy", self.rowp[0:1, 32:36], psP0[0:1, 32:36],
             inc=("vs", 1))
        r.wait("tensor", "vs")
        r.op("tensor", "matmul", psP0[:, 36:40], self.onesMs[:],
             self.rowp[:, 32:36], start=True, stop=True)
        r.inc_last("tensor", "ts")
        r.wait("vector", "ts")
        r.op("vector", "tensor_scalar", acc4[:, 0:1], psP0[:, 36:37],
             1.0 / (2048.0 * 6144.0), 1e-5, Alu.mult, Alu.max)
        r.op("vector", "tensor_scalar", acc4[:, 1:2], psP0[:, 37:38],
             1.0 / (2048.0 * 2048.0), 1e-5, Alu.mult, Alu.max)
        r.op("vector", "tensor_scalar", acc4[:, 2:3], psP0[:, 38:39],
             1.0 / (2048.0 * 2048.0), 1e-8, Alu.mult, Alu.max)
        r.op("vector", "reciprocal", self.qsc[:, 0:3], acc4[:, 0:3])
        r.op("vector", "tensor_scalar", self.isc[:, 0:3], acc4[:, 0:3], 0.5, None,
             Alu.mult)
        r.op("vector", "tensor_scalar", self.isc[:, 3:4], self.isc[:, 0:1], SCL,
             None, Alu.mult)
        r.op("vector", "memset", self.rowp[0:1, 32:36], 0.0, inc=("vs", 1))
        st0.close()
        r.barrier()
        stp.close()
        r.barrier()

    def p1(self):
        r = self.r
        self.hb_es = contextlib.ExitStack()
        self.hb = r.sb("hb", [128, NT * T], BF16, self.hb_es)
        st1 = contextlib.ExitStack()
        xt = r.sb("xt1", [128, 2 * T], F32, st1)
        xb = r.sb("xb1", [128, 2 * T], BF16, st1)
        sq = r.sb("sq1", [128, 2 * T], BF16, st1)
        ps1 = r.psum("ps11", [128, T], F32, st1)
        ps2 = r.psum("ps21", [128, T], F32, st1)
        self.ln_stats(self.x_in, ps1, ps2, xt, xb, sq)
        self.ln_finalize(ps1, ps2, 1e-5, EPSF)
        self.bcast_row(self.statr, T, self.RB, ps1)
        self.bcast_row(self.statr, 0, self.mRB, ps1)
        cons = [r.val["vs"]] * 2
        for kt in range(NT):
            b = (kt % 2) * T
            r.wait(SY, "vs", cons[kt % 2])
            ldx = r.dma(SY, xt[:, b:b + T], self.x_in.ap()[kt * 128:(kt + 1) * 128, :],
                        "ld")
            r.wait("vector", "ld", ldx)
            r.op("vector", "tensor_tensor", xt[:, b:b + T], xt[:, b:b + T],
                 self.RB[:], Alu.mult)
            r.op("vector", "tensor_tensor", xt[:, b:b + T], xt[:, b:b + T],
                 self.mRB[:], Alu.subtract)
            cons[kt % 2] = r.op("vector", "tensor_scalar",
                                self.hb[:, kt * T:(kt + 1) * T], xt[:, b:b + T],
                                self.nvs[:, kt:kt + 1], None, Alu.mult,
                                inc=("vs", 1))
        st1.close()
        r.barrier()

    def p3(self):
        r = self.r
        st3 = contextlib.ExitStack()
        wqh = r.sb("wqh", [128, 6144], F32, st3)
        terq = r.sb("terq", [128, 3 * T], BF16, st3)
        sg1 = r.sb("sg1", [128, T], BF16, st3)
        sg2 = r.sb("sg2", [128, T], BF16, st3)
        qh = r.sb("qh", [128, T], BF16, st3)
        kh = r.sb("kh", [128, T], BF16, st3)
        vh = r.sb("vh", [128, T], BF16, st3)
        obuf = r.sb("obuf", [128, T], F32, st3)
        osq = r.sb("osq", [128, T], BF16, st3)
        maskm = r.sb("maskm", [128, 128], F32, st3)
        lamq = r.sb("lamq", [128, 128], F32, st3)
        ltmp = r.sb("ltmp", [128, 1], F32, st3)
        lrev = r.sb("lrev", [128, 1], F32, st3)
        lTc = r.sb("lTc", [128, 1], F32, st3)
        Sst = r.sb("Sst", [128, 128], F32, st3)
        Sb = r.sb("Sb", [128, 128], BF16, st3)
        kTs = r.sb("kTs", [128, 128], BF16, st3)
        vTs = r.sb("vTs", [128, 128], BF16, st3)
        scb = r.sb("scb", [128, 128], BF16, st3)
        qlam = r.sb("qlam", [128, 128], BF16, st3)
        psA = r.psum("psA", [128, T], F32, st3)
        psB = r.psum("psB", [128, 1536], F32, st3)
        psT = r.psum("psT", [128, 1024], BF16, st3)

        r.op("vector", "memset", self.ost[:], 0.0, inc=("vs", 1))
        q_done = r.val["vs"]
        st_done = r.val["ss"]
        for h in range(H):
            r.wait(SY, "vs", q_done)
            ldw = None
            for wi in range(3):
                col = wi * 2048 + h * 128
                for kt in range(NT):
                    ldw = r.dma(
                        SY,
                        wqh[:, wi * 2048 + kt * 128:wi * 2048 + (kt + 1) * 128],
                        bass.AP(self.qkvT, kt * 128 * 6144 + col,
                                [[6144, 128], [1, 128]]),
                        "ld")
            r.wait("vector", "ld", ldw)
            for wi in range(3):
                sl = slice(wi * 2048, (wi + 1) * 2048)
                vsv = r.op("vector", "tensor_scalar", wqh[:, sl], wqh[:, sl],
                           self.qsc[:, 0:1], None, Alu.mult, inc=("vs", 1))
                r.wait("scalar", "vs", vsv)
                r.op("scalar", "activation", sg1[:], wqh[:, sl], Act.Sign,
                     bias=self.csts[:, C_MH:C_MH + 1])
                ssv = r.op("scalar", "activation", sg2[:], wqh[:, sl], Act.Sign,
                           bias=self.csts[:, C_PH:C_PH + 1], inc=("ss", 1))
                r.wait("vector", "ss", ssv)
                q_done = r.op("vector", "tensor_tensor", terq[:, sl], sg1[:],
                              sg2[:], Alu.add, inc=("vs", 1))
            sc_h = self.scol[:, h:h + 1]
            r.op("vector", "tensor_scalar", maskm[:], self.diffs[:], sc_h, None,
                 Alu.mult)
            r.op("vector", "tensor_scalar", ltmp[:], self.iotacs[:], -1.0, 128.0,
                 Alu.mult, Alu.add)
            r.op("vector", "tensor_scalar", ltmp[:], ltmp[:], sc_h, None, Alu.mult)
            r.op("vector", "tensor_scalar", lTc[:], sc_h, -128.0, None, Alu.mult,
                 inc=("vs", 1))
            r.wait("scalar", "vs")
            r.op("scalar", "activation", maskm[:], maskm[:], Act.Exp, scale=-1.0)
            r.op("scalar", "activation", lrev[:], ltmp[:], Act.Exp, scale=-1.0)
            r.op("scalar", "activation", lTc[:], lTc[:], Act.Exp)
            ssv = r.op("scalar", "activation",
                       bass.AP(self.rowp, 0, [[T, 1], [1, 128]]), self.niotas[:],
                       Act.Exp, scale=self.svrow[0:1, h:h + 1], inc=("ss", 1))
            r.wait("vector", "ss", ssv)
            r.op("vector", "tensor_tensor", maskm[:], maskm[:], self.tris[:],
                 Alu.mult, inc=("vs", 1))
            r.wait("tensor", "vs")
            r.op("tensor", "matmul", psA[:, 1536:1664], self.onesMs[:],
                 self.rowp[:, 0:128], start=True, stop=True)
            tsv = r.inc_last("tensor", "ts")
            r.wait("vector", "ts", tsv)
            r.op("vector", "tensor_copy", lamq[:], psA[:, 1536:1664], inc=("vs", 1))
            r.wait("tensor", "vs")
            r.wait("tensor", "ss", st_done)
            for tc in range(4):
                for wi in range(3):
                    psl = slice(wi * 512, wi * 512 + 512)
                    for kt in range(NT):
                        r.op("tensor", "matmul", psA[:, psl],
                             terq[:, wi * T + kt * 128:wi * T + kt * 128 + 128],
                             self.hb[:, kt * T + tc * 512:kt * T + tc * 512 + 512],
                             start=(kt == 0), stop=(kt == NT - 1))
                tsv = r.inc_last("tensor", "ts")
                r.wait("scalar", "ts", tsv)
                tsl = slice(tc * 512, (tc + 1) * 512)
                r.op("scalar", "activation", qh[:, tsl], psA[:, 0:512], Act.Copy,
                     scale=self.isc[:, 3:4])
                r.op("scalar", "activation", kh[:, tsl], psA[:, 512:1024],
                     Act.Copy, scale=self.isc[:, 0:1])
                st_done = r.op("scalar", "activation", vh[:, tsl],
                               psA[:, 1024:1536], Act.Copy, scale=self.isc[:, 0:1],
                               inc=("ss", 1))
                r.wait("tensor", "ss", st_done)
            r.op("vector", "memset", Sst[:], 0.0)
            r.op("vector", "memset", Sb[:], 0.0, inc=("vs", 1))
            r.wait("tensor", "vs")
            for ci in range(16):
                cs = slice(ci * 128, (ci + 1) * 128)
                r.op("tensor", "matmul", psB[:, 0:128], kh[:, cs], qh[:, cs],
                     start=True, stop=True)
                r.op("tensor", "transpose", psT[:, 0:128], kh[:, cs], self.idbs[:])
                tsv = r.inc_last("tensor", "ts")
                r.wait("vector", "ts", tsv)
                r.op("vector", "tensor_tensor", scb[:], psB[:, 0:128], maskm[:],
                     Alu.mult)
                r.op("vector", "tensor_tensor", qlam[:], qh[:, cs], lamq[:],
                     Alu.mult, inc=("vs", 1))
                r.wait("scalar", "ts", tsv)
                ssv = r.op("scalar", "activation", kTs[:], psT[:, 0:128], Act.Copy,
                           scale=lrev[:, 0:1], inc=("ss", 1))
                r.wait("tensor", "ss", ssv)
                r.op("tensor", "transpose", psT[:, 512:640], vh[:, cs],
                     self.idbs[:])
                tsv = r.inc_last("tensor", "ts")
                r.wait("scalar", "ts", tsv)
                ssv = r.op("scalar", "activation", vTs[:], psT[:, 512:640],
                           Act.Copy, inc=("ss", 1))
                r.wait("tensor", "ss", ssv)
                r.wait("tensor", "vs")
                r.op("tensor", "matmul", psB[:, 512:640], vTs[:], scb[:],
                     start=True, stop=False)
                r.op("tensor", "matmul", psB[:, 512:640], Sb[:], qlam[:],
                     start=False, stop=True)
                r.op("tensor", "matmul", psB[:, 1024:1152], kTs[:], vTs[:],
                     start=True, stop=True)
                tsv = r.inc_last("tensor", "ts")
                r.wait("scalar", "ts", tsv)
                r.op("scalar", "activation", obuf[:, cs], psB[:, 512:640],
                     Act.Copy, inc=("ss", 1))
                r.wait("vector", "ts", tsv)
                r.op("vector", "scalar_tensor_tensor", Sst[:], Sst[:], lTc[:, 0:1],
                     psB[:, 1024:1152], Alu.mult, Alu.add)
                r.op("vector", "tensor_copy", Sb[:], Sst[:], inc=("vs", 1))
                r.wait("tensor", "vs")
            ssv = r.val["ss"]
            r.wait("scalar", "ss", ssv)
            st_done = r.op("scalar", "activation", osq[:], obuf[:], Act.Square,
                           inc=("ss", 1))
            r.wait("tensor", "ss", st_done)
            for tc in range(4):
                r.op("tensor", "matmul", psA[0:1, tc * 512:(tc + 1) * 512],
                     self.onescb[:], osq[:, tc * 512:(tc + 1) * 512],
                     start=True, stop=True)
            tsv = r.inc_last("tensor", "ts")
            r.wait("vector", "ts", tsv)
            vsv = r.op("vector", "tensor_tensor", self.ost[0:1, 0:T],
                       self.ost[0:1, 0:T], psA[0:1, :], Alu.add, inc=("vs", 1))
            r.wait("tensor", "vs", vsv)
            for tc in range(4):
                r.op("tensor", "matmul", psA[0:1, tc * 512:(tc + 1) * 512],
                     self.w1sqb[:, h:h + 1], osq[:, tc * 512:(tc + 1) * 512],
                     start=True, stop=True)
            tsv = r.inc_last("tensor", "ts")
            r.wait("vector", "ts", tsv)
            q_done = r.op("vector", "tensor_tensor", self.ost[0:1, T:2 * T],
                          self.ost[0:1, T:2 * T], psA[0:1, :], Alu.add,
                          inc=("vs", 1))
            r.wait(SY, "ss", st_done)
            r.dma(SY, self.o0.ap()[h * 128:(h + 1) * 128, :], obuf[:], "sd")
            if h == 0 and "dbg" in self.expose:
                for dst, srcb in ((self.hbs.ap()[0:128, 0:4], self.isc[:]),
                                  (self.hbs.ap()[0:128, 4:8], self.qsc[:]),
                                  (self.hbs.ap()[0:128, 8:24], self.scol[:]),
                                  (self.hbs.ap()[0:128, 24:152], maskm[:]),
                                  (self.hbs.ap()[0:128, 152:280], lamq[:]),
                                  (self.hbs.ap()[0:128, 280:281], lrev[:]),
                                  (self.hbs.ap()[0:128, 281:282], lTc[:])):
                    r.op("gpsimd", "dma_start", inc=("gl", 16), out=dst,
                         in_=srcb, allow_slow_non_contiguous=True)
                r.dma("gpsimd", self.hbs.ap()[128:256, :], qh[:], "gl")
                r.dma("gpsimd", self.hbs.ap()[256:384, :], kh[:], "gl")
                r.dma("gpsimd", self.hbs.ap()[384:512, :], vh[:], "gl")
                r.dma("gpsimd", self.hbs.ap()[512:640, :], terq[:, 0:T], "gl")
                r.dma("gpsimd", self.hbs.ap()[640:768, :], obuf[:], "gl")
                r.barrier()
        st3.close()
        r.barrier()

    def p3b(self):
        r = self.r
        self.hb_es.close()
        st = contextlib.ExitStack()
        y2b = r.sb("y2b", [128, NT * T], BF16, st)
        otb = r.sb("otb", [128, 2 * T], F32, st)
        R0B = r.sb("R0B", [128, T], F32, st)
        wsp = r.sb("wsp", [128, 4096], F32, st)
        trp = r.sb("trp", [128, 4096], BF16, st)
        sp1 = r.sb("sp1", [128, 4096], BF16, st)
        sp2 = r.sb("sp2", [128, 4096], BF16, st)
        xrt = r.sb("xrt", [128, T], F32, st)
        psP = r.psum("psP", [128, 2048], F32, st)
        stt = self.statr

        r.op("vector", "tensor_scalar", stt[0:1, 0:T], self.ost[0:1, 0:T], 1.0 / D,
             1e-6, Alu.mult, Alu.add)
        rp = self.rowp
        r.op("vector", "reciprocal", rp[0:1, 0:T], stt[0:1, 0:T], inc=("vs", 1))
        r.wait("scalar", "vs")
        r.op("scalar", "activation", stt[0:1, 0:T], rp[0:1, 0:T], Act.Sqrt,
             inc=("ss", 1))
        r.wait("vector", "ss")
        r.op("vector", "tensor_tensor", stt[0:1, T:2 * T], stt[0:1, 0:T],
             stt[0:1, 0:T], Alu.mult)
        r.op("vector", "tensor_copy", rp[0:1, 0:T], self.ost[0:1, T:2 * T])
        r.op("vector", "tensor_scalar", rp[0:1, 0:T], rp[0:1, 0:T], 1.0 / D,
             None, Alu.mult)
        r.op("vector", "tensor_tensor", stt[0:1, T:2 * T], stt[0:1, T:2 * T],
             rp[0:1, 0:T], Alu.mult)
        r.op("vector", "tensor_scalar", stt[0:1, T:2 * T], stt[0:1, T:2 * T], 1.0,
             EPSF, Alu.mult, Alu.add)
        r.op("vector", "reciprocal", rp[0:1, 0:T], stt[0:1, T:2 * T], inc=("vs", 1))
        r.wait("scalar", "vs")
        r.op("scalar", "activation", stt[0:1, T:2 * T], rp[0:1, 0:T], Act.Sqrt,
             inc=("ss", 1))
        r.wait("vector", "ss")
        r.op("vector", "tensor_tensor", stt[0:1, 0:T], stt[0:1, 0:T], stt[0:1, T:2 * T],
             Alu.mult, inc=("vs", 1))
        self.bcast_row(stt, 0, R0B, psP)
        cons = [r.val["vs"]] * 2
        for kt in range(NT):
            b = (kt % 2) * T
            r.wait(SY, "vs", cons[kt % 2])
            ldo = r.dma(SY, otb[:, b:b + T], self.o0.ap()[kt * 128:(kt + 1) * 128, :],
                        "ld")
            r.wait("vector", "ld", ldo)
            r.op("vector", "tensor_tensor", otb[:, b:b + T], otb[:, b:b + T],
                 R0B[:], Alu.mult)
            cons[kt % 2] = r.op("vector", "tensor_scalar",
                                y2b[:, kt * T:(kt + 1) * T], otb[:, b:b + T],
                                self.w12s[:, kt:kt + 1], None, Alu.mult,
                                inc=("vs", 1))
        self._quant_mm_evac(self.projT, self.qsc[:, 1:2], self.isc[:, 1:2],
                            y2b, self.x_in, self.xres1, wsp, trp, sp1, sp2, xrt,
                            psP, beta_tile=None)
        st.close()
        self.r.barrier()
        self.ost_es.close()

    def _quant_mm_evac(self, wT, rscale_col, iscale_col, rhs_b, res_in, res_out,
                       wsp, trp, sp1, sp2, xrt, psP, beta_tile):
        r = self.r
        wq_done = r.val["vs"]
        for mg in range(8):
            r.wait(SY, "vs", wq_done)
            ldw = None
            for kt in range(NT):
                ldw = r.dma(SY,
                            wsp[:, kt * 256:(kt + 1) * 256],
                            bass.AP(wT, kt * 128 * 2048 + mg * 256,
                                    [[2048, 128], [1, 256]]),
                            "ld")
            r.wait("vector", "ld", ldw)
            vsv = r.op("vector", "tensor_scalar", wsp[:], wsp[:], rscale_col, None,
                       Alu.mult, inc=("vs", 1))
            r.wait("scalar", "vs", vsv)
            r.op("scalar", "activation", sp1[:], wsp[:], Act.Sign,
                 bias=self.csts[:, C_MH:C_MH + 1])
            ssv = r.op("scalar", "activation", sp2[:], wsp[:], Act.Sign,
                       bias=self.csts[:, C_PH:C_PH + 1], inc=("ss", 1))
            r.wait("vector", "ss", ssv)
            vsv = r.op("vector", "tensor_tensor", trp[:], sp1[:], sp2[:], Alu.add,
                       inc=("vs", 1))
            r.wait("tensor", "vs", vsv)
            evd = {0: 0, 1: 0}
            for mt in range(2):
                r.wait(SY, "sd")
                ldr = r.dma(SY, xrt[:, 0:T],
                            res_in.ap()[(mg * 2 + mt) * 128:(mg * 2 + mt + 1) * 128, :],
                            "ld")
                r.wait("vector", "ld", ldr)
                for tc in range(4):
                    psl = slice(mt * 1024 + (tc % 2) * 512,
                                mt * 1024 + (tc % 2) * 512 + 512)
                    r.wait("tensor", "vs", evd[tc % 2])
                    for kt in range(NT):
                        r.op("tensor", "matmul", psP[:, psl],
                             trp[:, kt * 256 + mt * 128:kt * 256 + mt * 128 + 128],
                             rhs_b[:, kt * T + tc * 512:kt * T + tc * 512 + 512],
                             start=(kt == 0), stop=(kt == NT - 1))
                    tsv = r.inc_last("tensor", "ts")
                    r.wait("vector", "ts", tsv)
                    sl2 = slice(tc * 512, tc * 512 + 512)
                    if beta_tile is None:
                        wq_done = r.op("vector", "scalar_tensor_tensor",
                                       xrt[:, sl2], psP[:, psl], iscale_col,
                                       xrt[:, sl2], Alu.mult, Alu.add,
                                       inc=("vs", 1))
                        evd[tc % 2] = wq_done
                    else:
                        r.op("vector", "tensor_tensor", sp1[:, 0:512], psP[:, psl],
                             bass.AP(beta_tile, tc * 512, [[T, 128], [1, 512]]),
                             Alu.mult) if False else None
                        r.op("vector", "tensor_tensor",
                             bass.AP(wsp, 0, [[4096, 128], [1, 512]]),
                             psP[:, psl],
                             bass.AP(beta_tile, tc * 512, [[T, 128], [1, 512]]),
                             Alu.mult)
                        wq_done = r.op("vector", "tensor_tensor", xrt[:, sl2],
                                       xrt[:, sl2],
                                       bass.AP(wsp, 0, [[4096, 128], [1, 512]]),
                                       Alu.add, inc=("vs", 1))
                    evd[tc % 2] = wq_done
                r.wait(SY, "vs", wq_done)
                m = mg * 2 + mt
                r.dma(SY, res_out.ap()[m * 128:(m + 1) * 128, :], xrt[:, 0:T], "sd")

    def p4(self):
        r = self.r
        st4 = contextlib.ExitStack()
        xqb = r.sb("xqb4", [128, NT * T], BF16, st4)
        abB = r.sb("abB4", [128, T], F32, st4)
        st4t = contextlib.ExitStack()
        xt = r.sb("xt4", [128, 2 * T], F32, st4t)
        xn = r.sb("xn4", [128, T], F32, st4t)
        mx = r.sb("mx4", [128, T], F32, st4t)
        rm = r.sb("rm4", [128, 16], F32, st4t)
        zbl = r.sb("zbl4", [128, T], F32, st4t)
        btr = r.sb("btr4", [16, 128], F32, st4t)
        bbB = r.sb("bbB4", [128, T], F32, st4t)
        rbB = bbB
        xqi = r.sb("xqi4", [128, T], I16, st4t)
        gq = mx
        gq2 = zbl

        stS = contextlib.ExitStack()
        xb = r.sb("xb4", [128, 2 * T], BF16, stS)
        sq = r.sb("sq4", [128, 2 * T], BF16, stS)
        ps1 = r.psum("ps14", [128, T], F32, stS)
        ps2 = r.psum("ps24", [128, T], F32, stS)
        self.ln_stats(self.xres1, ps1, ps2, xt, xb, sq)
        self.ln_finalize(ps1, ps2, 1e-5, None)
        self.bcast_row(self.statr, T, self.RB, ps1)
        self.bcast_row(self.statr, 0, self.mRB, ps1)
        r.op("vector", "memset", mx[:], 0.0, inc=("vs", 1))
        cons = [r.val["vs"]] * 2
        for kt in range(NT):
            b = (kt % 2) * T
            r.wait(SY, "vs", cons[kt % 2])
            ldx = r.dma(SY, xt[:, b:b + T],
                        self.xres1.ap()[kt * 128:(kt + 1) * 128, :], "ld")
            r.wait("vector", "ld", ldx)
            r.op("vector", "tensor_tensor", xn[:], xt[:, b:b + T], self.RB[:],
                 Alu.mult)
            r.op("vector", "tensor_tensor", xn[:], xn[:], self.mRB[:],
                 Alu.subtract)
            r.op("vector", "tensor_tensor", mx[:], mx[:], xn[:], Alu.max)
            cons[kt % 2] = r.op("vector", "scalar_tensor_tensor", mx[:], xn[:],
                                -1.0, mx[:], Alu.mult, Alu.max, inc=("vs", 1))
        r.mark("p4beta")
        vsv = r.val["vs"]
        r.wait("tensor", "vs", vsv)
        for tc in range(16):
            r.op("tensor", "transpose", ps1[:, 0:128],
                 mx[:, tc * 128:(tc + 1) * 128], self.idfs[:])
            tsv = r.inc_last("tensor", "ts")
            r.wait("vector", "ts", tsv)
            vsv = r.op("vector", "reduce_max", rm[:, tc:tc + 1], ps1[:, 0:128], AX,
                       inc=("vs", 1))
            r.wait("tensor", "vs", vsv)
        r.op("tensor", "transpose", ps2[0:16, 0:128], rm[:], self.idfs[:])
        tsv = r.inc_last("tensor", "ts")
        r.wait("vector", "ts", tsv)
        r.op("vector", "tensor_copy", btr[:], ps2[0:16, 0:128])
        vsv = r.op("vector", "memset", zbl[:], 0.0, inc=("vs", 1))
        r.wait(SY, "vs", vsv)
        for tc in range(16):
            r.dma(SY, zbl[tc:tc + 1, tc * 128:(tc + 1) * 128],
                  btr[tc:tc + 1, 0:128], "ld")
        vsv = r.wait("vector", "ld")
        vsv = r.op("vector", "tensor_copy", btr[0:1, 0:1], btr[0:1, 0:1],
                   inc=("vs", 1))
        r.wait("tensor", "vs", vsv)
        for tc in range(4):
            sl = slice(tc * 512, (tc + 1) * 512)
            r.op("tensor", "matmul", ps1[:, sl], self.onesMs[:], zbl[:, sl],
                 start=True, stop=True)
        tsv = r.inc_last("tensor", "ts")
        r.wait("vector", "ts", tsv)
        r.op("vector", "tensor_scalar", bbB[:], ps1[:], 1.0 / 127.0, 1e-8,
             Alu.mult, Alu.max)
        r.op("vector", "tensor_scalar", abB[:], bbB[:], self.isc[:, 2:3], None,
             Alu.mult)
        r.op("vector", "reciprocal", rbB[:], bbB[:], inc=("vs", 1))
        cons = [r.val["vs"]] * 2
        for kt in range(NT):
            b = (kt % 2) * T
            r.wait(SY, "vs", cons[kt % 2])
            ldx = r.dma(SY, xt[:, b:b + T],
                        self.xres1.ap()[kt * 128:(kt + 1) * 128, :], "ld")
            r.wait("vector", "ld", ldx)
            r.op("vector", "tensor_tensor", xn[:], xt[:, b:b + T], self.RB[:],
                 Alu.mult)
            r.op("vector", "tensor_tensor", xn[:], xn[:], self.mRB[:],
                 Alu.subtract)
            r.op("vector", "tensor_tensor", xn[:], xn[:], rbB[:], Alu.mult)
            r.op("vector", "tensor_scalar", xqi[:], xn[:], 1.0, None, Alu.mult)
            r.op("vector", "scalar_tensor_tensor", gq[:], xqi[:], -1.0, xn[:],
                 Alu.mult, Alu.add)
            r.op("vector", "tensor_scalar", gq2[:], gq[:], -0.5, None, Alu.is_le)
            r.op("vector", "tensor_scalar", gq[:], gq[:], 0.5, None, Alu.is_ge)
            r.op("vector", "tensor_tensor", gq[:], gq[:], xqi[:], Alu.add)
            r.op("vector", "tensor_tensor", gq[:], gq[:], gq2[:], Alu.subtract)
            cons[kt % 2] = r.op("vector", "tensor_scalar",
                                xqb[:, kt * T:(kt + 1) * T], gq[:], -127.0, 127.0,
                                Alu.max, Alu.min, inc=("vs", 1))
        stS.close()
        r.barrier()
        st4t.close()
        r.barrier()
        st4b = contextlib.ExitStack()
        wsp = r.sb("wsp4", [128, 4096], F32, st4b)
        trp = r.sb("trp4", [128, 4096], BF16, st4b)
        sp1 = r.sb("sp14", [128, 4096], BF16, st4b)
        sp2 = r.sb("sp24", [128, 4096], BF16, st4b)
        xrt = r.sb("xrt4", [128, T], F32, st4b)
        psP = r.psum("psP4", [128, 2048], F32, st4b)
        self._quant_mm_evac(self.ffn0T, self.qsc[:, 2:3], None, xqb, self.xres1,
                            self.xres2, wsp, trp, sp1, sp2, xrt, psP,
                            beta_tile=abB)
        st4b.close()
        r.barrier()
        st4.close()

    def p5(self):
        r = self.r
        st5 = contextlib.ExitStack()
        xt = r.sb("xt5", [128, 2 * T], F32, st5)
        xlnb = r.sb("xlnb5", [128, NT * T], BF16, st5)

        stA = contextlib.ExitStack()
        xb = r.sb("xb5", [128, 2 * T], BF16, stA)
        sq = r.sb("sq5", [128, 2 * T], BF16, stA)
        xnf = r.sb("xnf5", [128, T], F32, stA)
        ps1 = r.psum("ps15", [128, T], F32, stA)
        ps2 = r.psum("ps25", [128, T], F32, stA)
        self.ln_stats(self.xres2, ps1, ps2, xt, xb, sq)
        self.ln_finalize(ps1, ps2, 1e-5, None)
        self.bcast_row(self.statr, T, self.RB, ps1)
        self.bcast_row(self.statr, 0, self.mRB, ps1)
        cons = [r.val["vs"]] * 2
        for kt in range(NT):
            b = (kt % 2) * T
            r.wait(SY, "vs", cons[kt % 2])
            ldx = r.dma(SY, xt[:, b:b + T],
                        self.xres2.ap()[kt * 128:(kt + 1) * 128, :], "ld")
            r.wait("vector", "ld", ldx)
            r.op("vector", "tensor_tensor", xnf[:], xt[:, b:b + T], self.RB[:],
                 Alu.mult)
            r.op("vector", "tensor_tensor", xnf[:], xnf[:], self.mRB[:],
                 Alu.subtract)
            r.op("vector", "tensor_copy", xlnb[:, kt * T:(kt + 1) * T], xnf[:])
            cons[kt % 2] = r.op("vector", "tensor_scalar", xt[:, b:b + T], xnf[:],
                                self.gsig1m[:, kt:kt + 1], None, Alu.mult,
                                inc=("vs", 1))
            r.wait(SY, "vs", cons[kt % 2])
            r.dma(SY, self.vg1s.ap()[kt * 128:(kt + 1) * 128, :], xt[:, b:b + T],
                  "sd")
        stA.close()
        r.barrier()
        stSH = contextlib.ExitStack()
        shb = r.sb("shb5", [128, NT * T], BF16, stSH)
        gl = None
        for kt in range(NT):
            gl = r.dma("gpsimd", shb[:, kt * T:(kt + 1) * T],
                       self.shiftT.ap()[kt * 128:(kt + 1) * 128, :], "gl")
        stB = contextlib.ExitStack()
        psSH = r.psum("psSH5", [128, 1024], F32, stB)
        r.wait("tensor", "gl", gl)
        ev_done = r.val["vs"]
        for m in range(NT):
            r.wait(SY, "vs", ev_done)
            mb = (m % 2) * T
            ldv = r.dma(SY, xt[:, mb:mb + T],
                        self.vg1s.ap()[m * 128:(m + 1) * 128, :], "ld")
            r.wait("vector", "ld", ldv)
            for tc in range(4):
                psl = slice((tc % 2) * 512, (tc % 2) * 512 + 512)
                src = (tc + 2) % 4
                for kt in range(NT):
                    r.op("tensor", "matmul", psSH[:, psl],
                         shb[:, kt * T + m * 128:kt * T + m * 128 + 128],
                         xlnb[:, kt * T + src * 512:kt * T + src * 512 + 512],
                         start=(kt == 0), stop=(kt == NT - 1))
                tsv = r.inc_last("tensor", "ts")
                r.wait("vector", "ts", tsv)
                sl2 = slice(mb + tc * 512, mb + tc * 512 + 512)
                ev_done = r.op("vector", "scalar_tensor_tensor", xt[:, sl2],
                               psSH[:, psl], self.gsig[:, m:m + 1], xt[:, sl2],
                               Alu.mult, Alu.add, inc=("vs", 1))
                r.wait("tensor", "vs", ev_done)
            r.wait(SY, "vs", ev_done)
            r.dma(SY, self.xshs.ap()[m * 128:(m + 1) * 128, :], xt[:, mb:mb + T],
                  "sd")
        stB.close()
        r.barrier()
        r.wait(SY, "sd")
        stSH.close()
        r.barrier()
        r.mark("p5ln2")
        stC = contextlib.ExitStack()
        xb = r.sb("xb5b", [128, 2 * T], BF16, stC)
        sq = r.sb("sq5b", [128, 2 * T], BF16, stC)
        xnf = r.sb("xnf5b", [128, T], F32, stC)
        ps1 = r.psum("ps15b", [128, T], F32, stC)
        ps2 = r.psum("ps25b", [128, T], F32, stC)
        self.ln_stats(self.xshs, ps1, ps2, xt, xb, sq)
        self.ln_finalize(ps1, ps2, 1e-5, None)
        self.bcast_row(self.statr, T, self.RB, ps1)
        self.bcast_row(self.statr, 0, self.mRB, ps1)
        xnb = xlnb
        cons = [r.val["vs"]] * 2
        for kt in range(NT):
            b = (kt % 2) * T
            r.wait(SY, "vs", cons[kt % 2])
            ldx = r.dma(SY, xt[:, b:b + T],
                        self.xshs.ap()[kt * 128:(kt + 1) * 128, :], "ld")
            r.wait("vector", "ld", ldx)
            r.op("vector", "tensor_tensor", xnf[:], xt[:, b:b + T], self.RB[:],
                 Alu.mult)
            r.op("vector", "tensor_tensor", xnf[:], xnf[:], self.mRB[:],
                 Alu.subtract)
            cons[kt % 2] = r.op("vector", "tensor_scalar",
                                xnb[:, kt * T:(kt + 1) * T], xnf[:],
                                self.nvs[:, 48 + kt:48 + kt + 1],
                                self.nvs[:, 64 + kt:64 + kt + 1], Alu.mult,
                                Alu.add, inc=("vs", 1))
        stC.close()
        r.barrier()
        r.mark("p5kv")
        stD = contextlib.ExitStack()
        wkb = r.sb("wkb5", [128, NT * 128], BF16, stD)
        kk = r.sb("kk5", [64, T], F32, stD)
        vv = r.sb("vv5", [64, T], F32, stD)
        wkv = r.sb("wkv5", [64, T], F32, stD)
        wrow = r.sb("wrow5", [64, T], F32, stD)
        states = r.sb("states5", [64, T], F32, stD)
        stb = r.sb("stb5", [64, T], BF16, stD)
        wob = r.sb("wob5", [64, NT * 128], BF16, stD)
        psKV = r.psum("psKV5", [128, 1024], F32, stD)
        gl = None
        for kt in range(NT):
            gl = r.dma("gpsimd", wkb[:, kt * 128:(kt + 1) * 128],
                       self.wkvT.ap()[kt * 128:(kt + 1) * 128, :], "gl")
        r.wait("tensor", "gl", gl)
        r.wait("tensor", "vs")
        for tc in range(4):
            sl = slice(tc * 512, (tc + 1) * 512)
            for kt in range(NT):
                r.op("tensor", "matmul", psKV[0:64, 0:512],
                     wkb[:, kt * 128:kt * 128 + 64],
                     xnb[:, kt * T + tc * 512:kt * T + tc * 512 + 512],
                     start=(kt == 0), stop=(kt == NT - 1))
                r.op("tensor", "matmul", psKV[0:64, 512:1024],
                     wkb[:, kt * 128 + 64:kt * 128 + 128],
                     xnb[:, kt * T + tc * 512:kt * T + tc * 512 + 512],
                     start=(kt == 0), stop=(kt == NT - 1))
            tsv = r.inc_last("tensor", "ts")
            r.wait("scalar", "ts", tsv)
            r.op("scalar", "activation", kk[:, sl], psKV[0:64, 0:512], Act.Copy)
            ssv = r.op("scalar", "activation", vv[:, sl], psKV[0:64, 512:1024],
                       Act.Copy, inc=("ss", 1))
            r.wait("tensor", "ss", ssv)
        r.wait("scalar", "ss")
        ssv = r.op("scalar", "activation", wkv[:], kk[:], Act.Sigmoid,
                   inc=("ss", 1))
        r.wait("vector", "ss", ssv)
        r.op("vector", "tensor_scalar", wkv[:], wkv[:], self.ucolv[:, 0:1], None,
             Alu.mult, inc=("vs", 1))
        r.wait("scalar", "vs")
        ssv = r.op("scalar", "activation", wkv[:], wkv[:], Act.Exp, scale=-1.0,
                   inc=("ss", 1))
        r.wait("vector", "ss", ssv)
        r.op("vector", "tensor_tensor", wkv[:], wkv[:], vv[:], Alu.mult)
        r.op("vector", "memset", wrow[:], 1.0)
        r.op("vector", "tensor_scalar", wrow[:], wrow[:], self.wcolv[:, 0:1],
             None, Alu.mult)
        r.op("vector", "tensor_tensor_scan", states[:, 0:1024], wrow[:, 0:1024],
             wkv[:, 0:1024], 0.0, Alu.mult, Alu.add)
        r.op("vector", "drain")
        r.op("vector", "tensor_tensor_scan", states[:, 1024:2048],
             wrow[:, 1024:2048], wkv[:, 1024:2048], states[:, 1023:1024],
             Alu.mult, Alu.add)
        r.op("vector", "tensor_copy", stb[:], states[:], inc=("vs", 1))
        if "dbg5" in self.expose:
            vsd = r.val["vs"]
            r.wait("gpsimd", "vs", vsd)
            r.dma("gpsimd", self.hbs.ap()[0:64, :], kk[:], "gl")
            r.dma("gpsimd", self.hbs.ap()[64:128, :], vv[:], "gl")
            r.dma("gpsimd", self.hbs.ap()[128:192, :], wkv[:], "gl")
            r.dma("gpsimd", self.hbs.ap()[192:256, :], states[:], "gl")
            r.dma("gpsimd", self.hbs.ap()[256:320, :], stb[:], "gl")
            r.wait(SY, "gl")
            r.barrier()
        gl = None
        for kt in range(NT):
            gl = r.dma("gpsimd", wob[:, kt * 128:(kt + 1) * 128],
                       self.woutT.ap()[0:64, kt * 128:(kt + 1) * 128], "gl")
        r.wait("tensor", "gl", gl)
        r.wait("tensor", "vs")
        ev_done = r.val["vs"]
        for mt in range(NT):
            r.wait(SY, "vs", ev_done)
            ldr = r.dma(SY, xt[:, 0:T], self.xres2.ap()[mt * 128:(mt + 1) * 128, :],
                        "ld")
            r.wait("vector", "ld", ldr)
            for tc in range(4):
                r.op("tensor", "matmul", psKV[:, 0:512],
                     wob[:, mt * 128:(mt + 1) * 128],
                     stb[:, tc * 512:(tc + 1) * 512], start=True, stop=True)
                tsv = r.inc_last("tensor", "ts")
                r.wait("vector", "ts", tsv)
                ev_done = r.op("vector", "tensor_tensor",
                               xt[:, tc * 512:(tc + 1) * 512],
                               xt[:, tc * 512:(tc + 1) * 512], psKV[:, 0:512],
                               Alu.add, inc=("vs", 1))
                r.wait("tensor", "vs", ev_done)
            r.wait(SY, "vs", ev_done)
            r.dma(SY, self.xres3.ap()[mt * 128:(mt + 1) * 128, :], xt[:, 0:T], "sd")
        stD.close()
        r.barrier()
        st5.close()

    def p6(self):
        r = self.r
        stq = contextlib.ExitStack()
        SB = 2048
        pi = r.sb("pi6", [128, 2 * SB], U8, stq)
        scd = r.sb("scd6", [128, 2 * (SB // 16)], F32, stq)
        aL = r.sb("aL6", [128, 2 * 2 * SB], U8, stq)
        sL = r.sb("sL6", [128, 2 * 2 * SB], U8, stq)
        taL = r.sb("taL6", [128, 2 * 2 * SB], BF16, stq)
        r1L = r.sb("r1L6", [128, 2 * 2 * SB], BF16, stq)
        r2L = r.sb("r2L6", [128, 2 * 2 * SB], BF16, stq)
        m2L = r.sb("m2L6", [128, 2 * SB], BF16, stq)
        sfL = r.sb("sfL6", [128, 2 * 2 * SB], BF16, stq)
        dec = r.sb("dec6", [128, 2 * 2 * SB], BF16, stq)

        def dequant(src_u8, src_sc, dst, nrows, nbytes):
            ntile = nrows // 128
            nsub = max(1, nbytes // SB)
            sb = min(nbytes, SB)
            nblk = sb // 16
            pcons = [r.val["vs"]] * 2
            wcons = [r.val["vs"]] * 2
            acons = [r.val["ss"]] * 2
            stq_q = (("scalar", "sd"), ("gpsimd", "sg"))
            dcons = [("sd", r.val["sd"])] * 2
            pend = None

            def front(it0):
                it, su = it0 // nsub, it0 % nsub
                sl = it0 % 2
                b = sl * SB
                bs = sl * (SB // 16)
                r.wait(SY, "vs", max(pcons[sl], wcons[sl]))
                r.dma(SY, pi[:, b:b + sb],
                      bass.AP(src_u8, it * 128 * nbytes + su * sb,
                              [[nbytes, 128], [1, sb]]), "ld")
                ldc = r.dma(SY, scd[:, bs:bs + nblk],
                            bass.AP(src_sc, it * 128 * (nbytes // 16) + su * nblk,
                                    [[nbytes // 16, 128], [1, nblk]]), "ld")
                sb2 = sl * 2 * SB
                r.wait("vector", "ld", ldc)
                r.wait("vector", "ss", acons[sl])
                r.op("vector", "tensor_scalar", aL[:, sb2:sb2 + sb],
                     pi[:, b:b + sb], 7, None, Alu.bitwise_and)
                r.op("vector", "tensor_scalar", aL[:, sb2 + sb:sb2 + 2 * sb],
                     pi[:, b:b + sb], 4, 7, Alu.logical_shift_right,
                     Alu.bitwise_and)
                r.op("vector", "tensor_scalar", sL[:, sb2:sb2 + sb],
                     pi[:, b:b + sb], 3, 1, Alu.logical_shift_right,
                     Alu.bitwise_and)
                r.op("vector", "tensor_scalar", sL[:, sb2 + sb:sb2 + 2 * sb],
                     pi[:, b:b + sb], 7, None, Alu.logical_shift_right)
                vsx = r.op("vector", "tensor_scalar", sfL[:, sb2:sb2 + 2 * sb],
                           sL[:, sb2:sb2 + 2 * sb], -2.0, 1.0, Alu.mult,
                           Alu.add, inc=("vs", 1))
                pcons[sl] = vsx
                r.wait("scalar", "vs", vsx)
                r.op("scalar", "activation", taL[:, sb2:sb2 + 2 * sb],
                     aL[:, sb2:sb2 + 2 * sb], Act.Copy, scale=0.5)
                r.op("scalar", "activation", r1L[:, sb2:sb2 + 2 * sb],
                     aL[:, sb2:sb2 + 2 * sb], Act.Relu,
                     bias=self.csts[:, C_M2:C_M2 + 1], scale=0.5)
                acons[sl] = r.op("scalar", "activation",
                                 r2L[:, sb2:sb2 + 2 * sb],
                                 aL[:, sb2:sb2 + 2 * sb], Act.Relu,
                                 bias=self.csts[:, C_M6:C_M6 + 1], scale=1.0,
                                 inc=("ss", 1))
                return (sl, it, su, acons[sl])

            def back(pend_t):
                sl, it, su, ssv = pend_t
                b2 = sl * 2 * SB
                bs = sl * (SB // 16)
                bd = sl * 2 * SB
                r.wait("vector", "ss", ssv)
                r.op("vector", "tensor_tensor", m2L[:, 0:2 * sb],
                     taL[:, b2:b2 + 2 * sb], r1L[:, b2:b2 + 2 * sb], Alu.add)
                r.op("vector", "tensor_tensor", m2L[:, 0:2 * sb],
                     m2L[:, 0:2 * sb], r2L[:, b2:b2 + 2 * sb], Alu.add)
                r.op("vector", "tensor_tensor", m2L[:, 0:2 * sb],
                     m2L[:, 0:2 * sb], sfL[:, b2:b2 + 2 * sb], Alu.mult)
                r.wait("vector", dcons[sl][0], dcons[sl][1])
                for half in range(2):
                    vsw = r.op(
                        "vector", "tensor_tensor",
                        bass.AP(dec, bd + half * 64,
                                [[2 * 2 * SB, 128], [128, sb // 64], [1, 64]]),
                        bass.AP(m2L, half * sb, [[2 * SB, 128], [64, sb // 64], [1, 64]]),
                        bass.AP(scd, bs, [[2 * (SB // 16), 128], [1, nblk], [0, 16]]),
                        Alu.mult, inc=("vs", 1))
                wcons[sl] = vsw
                qeng, qsem = stq_q[(it * nsub + su) % 2]
                r.wait(qeng, "vs", vsw)
                dcons[sl] = (qsem, r.dma(
                    qeng, bass.AP(dst, it * 128 * 2 * nbytes + su * 2 * sb,
                                  [[2 * nbytes, 128], [1, 2 * sb]]),
                    dec[:, bd:bd + 2 * sb], qsem))

            for it0 in range(ntile * nsub):
                t = front(it0)
                if pend is not None:
                    back(pend)
                pend = t
            back(pend)

        r.mark("p6deq")
        dequant(self.gp_u8, self.gsc, self.wdg, MLP // 2, 2048)
        dequant(self.up_u8, self.usc, self.wdu, MLP // 2, 2048)
        dequant(self.dn_u8, self.dsc, self.wdd, D, 8192)
        stq.close()
        r.barrier()

        r.mark("p6b")
        st6 = contextlib.ExitStack()
        acc = r.sb("acc6", [128, NT * TH], F32, st6)
        xph = r.sb("xph6", [128, NT * TH], BF16, st6)

        stS = contextlib.ExitStack()
        xt = r.sb("xt6", [128, 2 * T], F32, stS)
        xb = r.sb("xb6", [128, TH], BF16, stS)
        sq6 = r.sb("sq6", [128, TH], BF16, stS)
        xnf = r.sb("xnf6", [128, TH], F32, stS)
        ps1 = r.psum("ps16", [128, TH], F32, stS)
        ps2 = r.psum("ps26", [128, TH], F32, stS)
        cons = [r.val["vs"]] * 2
        scons = [r.val["ss"]] * 2
        tprev = r.val["ts"]
        for kt in range(NT):
            b = (kt % 2) * T
            r.wait(SY, "vs", cons[kt % 2])
            r.wait(SY, "ss", scons[kt % 2])
            ldx = r.dma(SY, xt[:, b:b + T],
                        self.xres3.ap()[kt * 128:(kt + 1) * 128, :], "ld")
            r.wait("vector", "ld", ldx)
            asl = slice(kt * TH, (kt + 1) * TH)
            r.op("vector", "tensor_scalar", acc[:, asl], xt[:, b:b + TH],
                 self.pcols[:, 1:2], None, Alu.mult)
            r.op("vector", "scalar_tensor_tensor", acc[:, asl],
                 xt[:, b + TH:b + T], self.pcols[:, 0:1], acc[:, asl],
                 Alu.mult, Alu.add)
            r.wait("vector", "ts", tprev)
            vsv = r.op("vector", "tensor_copy", xb[:], acc[:, asl], inc=("vs", 1))
            cons[kt % 2] = vsv
            r.wait("scalar", "vs", vsv)
            r.wait("scalar", "ts", tprev)
            ssv = r.op("scalar", "activation", sq6[:], acc[:, asl], Act.Square,
                       inc=("ss", 1))
            scons[kt % 2] = ssv
            r.wait("tensor", "vs", vsv)
            r.wait("tensor", "ss", ssv)
            for tc in range(2):
                sl = slice(tc * 512, (tc + 1) * 512)
                r.op("tensor", "matmul", ps1[0:1, sl], self.onescb[:], xb[:, sl],
                     start=(kt == 0), stop=(kt == NT - 1))
                r.op("tensor", "matmul", ps2[0:1, sl], self.onescb[:], sq6[:, sl],
                     start=(kt == 0), stop=(kt == NT - 1))
            tprev = r.inc_last("tensor", "ts")
        self.ln_finalize(ps1, ps2, 1e-5, None, tok=TH)
        self.bcast_row(self.statr, T, self.RB, ps1, tok=TH)
        self.bcast_row(self.statr, 0, self.mRB, ps1, tok=TH)
        for kt in range(NT):
            asl = slice(kt * TH, (kt + 1) * TH)
            r.op("vector", "tensor_tensor", xnf[:], acc[:, asl], self.RB[:, 0:TH],
                 Alu.mult)
            r.op("vector", "tensor_tensor", xnf[:], xnf[:], self.mRB[:, 0:TH],
                 Alu.subtract)
            vsv = r.op("vector", "tensor_copy", xb[:], xnf[:], inc=("vs", 1))
            r.wait("tensor", "vs", vsv)
            for tc in range(2):
                sl = slice(tc * 512, (tc + 1) * 512)
                r.op("tensor", "matmul", ps1[:, sl], self.perms[:], xb[:, sl],
                     start=True, stop=True)
            tsv = r.inc_last("tensor", "ts")
            r.wait("vector", "ts", tsv)
            r.op("vector", "tensor_copy", xph[:, asl], ps1[:], inc=("vs", 1))
        stS.close()
        r.barrier()

        r.mark("p6c")
        stC = contextlib.ExitStack()
        wst = r.sb("wst6", [128, 2 * 8192], BF16, stC)
        gbuf = r.sb("gbuf6", [128, 4 * TH], BF16, stC)
        itb = r.sb("itb6", [128, 8 * TH], BF16, stC)
        psG = r.psum("psG6", [128, 2048], F32, stC)
        psD = r.psum("psD6", [128, 2048], F32, stC)

        r.wait(SY, "sd")
        r.wait(SY, "sg")
        slotcons = [r.val["ts"]] * 2
        it_done = r.val["vs"]
        dn_done = r.val["vs"]
        gss = r.val["ss"]
        si = 0

        def mm_stage(mat, nc5):
            nonlocal si, it_done, gss
            s = si % 2
            sb_ = s * 8192
            si += 1
            src = self.wdg if mat == "g" else self.wdu
            r.wait(SY, "ts", slotcons[s])
            ldv = None
            for kt in range(NT):
                ldv = r.dma(SY, wst[:, sb_ + kt * 512:sb_ + kt * 512 + 512],
                            src.ap()[nc5 * 512:(nc5 + 1) * 512,
                                     kt * 128:(kt + 1) * 128],
                            "ld", transpose=True)
            r.wait("tensor", "ld", ldv)
            r.wait("tensor", "ss", gss)
            r.wait("tensor", "vs", it_done)
            if mat == "u":
                r.wait("vector", "ss", gss)
            else:
                r.wait("scalar", "vs", it_done)
            init = gss if mat == "g" else it_done
            evg = {0: init, 1: init}
            first = {0: True, 1: True}
            for nt in range(4):
                for tc in range(2):
                    g = (2 * nt + tc) % 2
                    gb = g * 1024
                    if not first[g]:
                        r.wait("tensor", "vs" if mat == "u" else "ss", evg[g])
                    first[g] = False
                    for kt in range(NT):
                        r.op("tensor", "matmul", psG[:, gb:gb + 512],
                             wst[:, sb_ + kt * 512 + nt * 128:
                                 sb_ + kt * 512 + nt * 128 + 128],
                             xph[:, kt * TH + tc * 512:kt * TH + tc * 512 + 512],
                             start=(kt == 0), stop=(kt == NT - 1))
                    tsv = r.inc_last("tensor", "ts")
                    if mat == "g":
                        r.wait("scalar", "ts", tsv)
                        gss = r.op("scalar", "activation",
                                   gbuf[:, nt * TH + tc * 512:nt * TH + tc * 512 + 512],
                                   psG[:, gb:gb + 512], Act.Gelu_apprx_tanh,
                                   inc=("ss", 1))
                        evg[g] = gss
                    else:
                        r.wait("vector", "ts", tsv)
                        it_done = r.op(
                            "vector", "tensor_tensor",
                            itb[:, ((nc5 % 2) * 4 + nt) * TH + tc * 512:
                                ((nc5 % 2) * 4 + nt) * TH + tc * 512 + 512],
                            gbuf[:, nt * TH + tc * 512:nt * TH + tc * 512 + 512],
                            psG[:, gb:gb + 512], Alu.mult, inc=("vs", 1))
                        evg[g] = it_done
            slotcons[s] = r.val["ts"]

        for nc5 in range(32):
            mm_stage("g", nc5)
            mm_stage("u", nc5)
            if nc5 % 2 == 1:
                r.wait(SY, "ts", slotcons[0])
                r.wait(SY, "ts", slotcons[1])
                si += si % 2
                ldd = None
                for j in range(8):
                    nglob = (nc5 - 1) * 512 + j * 128
                    ldd = r.dma(SY, wst[:, j * 2048:(j + 1) * 2048],
                                self.wdd.ap()[0:D, nglob:nglob + 128],
                                "ld", transpose=True)
                r.wait("tensor", "ld", ldd)
                r.wait("tensor", "vs", it_done)
                evd = {0: dn_done, 1: dn_done}
                dfirst = {0: True, 1: True}
                for mt in range(NT):
                    g = mt % 2
                    db = g * 1024
                    if not dfirst[g]:
                        r.wait("tensor", "vs", evd[g])
                    dfirst[g] = False
                    for tc in range(2):
                        for j in range(8):
                            r.op("tensor", "matmul",
                                 psD[:, db + tc * 512:db + (tc + 1) * 512],
                                 wst[:, j * 2048 + mt * 128:j * 2048 + mt * 128 + 128],
                                 itb[:, j * TH + tc * 512:j * TH + tc * 512 + 512],
                                 start=(j == 0), stop=(j == 7))
                    tsv = r.inc_last("tensor", "ts")
                    r.wait("vector", "ts", tsv)
                    dn_done = r.op("vector", "tensor_tensor",
                                   acc[:, mt * TH:(mt + 1) * TH],
                                   acc[:, mt * TH:(mt + 1) * TH],
                                   psD[:, db:db + 1024],
                                   Alu.add, inc=("vs", 1))
                    evd[g] = dn_done
                slotcons[0] = r.val["ts"]
                slotcons[1] = r.val["ts"]
        for mt in range(NT):
            r.wait(SY, "vs", dn_done)
            r.dma(SY, self.out_t.ap()[mt * 128:(mt + 1) * 128, :],
                  acc[:, mt * TH:(mt + 1) * TH], "so")
        r.wait(SY, "so")
        stC.close()
        r.barrier()
        st6.close()


def build_program(upto=9, expose=()):
    kb = KB(expose=expose)
    kb.r.mark("p0")
    kb.p0()
    if upto >= 1:
        kb.r.barrier()
        kb.r.mark("p1")
        kb.p1()
        if "hbs" in expose:
            for kt in range(NT):
                kb.r.dma("gpsimd", kb.hbs.ap()[kt * 128:(kt + 1) * 128, :],
                         kb.hb[:, kt * T:(kt + 1) * T], "gl")
            kb.r.barrier()
    if upto >= 2:
        kb.r.barrier()
        kb.r.mark("p3")
        kb.p3()
    if upto >= 3:
        kb.r.barrier()
        kb.r.mark("p3b")
        kb.p3b()
    if upto >= 4:
        kb.r.barrier()
        kb.r.mark("p4")
        kb.p4()
    if upto >= 5:
        kb.r.barrier()
        kb.r.mark("p5")
        kb.p5()
    if upto >= 6:
        kb.r.barrier()
        kb.r.mark("p6")
        kb.p6()
    else:
        kb.epilogue_zero()
    return kb.r.build()


def _host_inputs(inputs):
    f32 = np.float32
    x = np.asarray(inputs["L0_attn_x"], f32)
    qkv_w = np.asarray(inputs["L0_attn_qkv_w"], f32)
    proj_w = np.asarray(inputs["L0_attn_proj_w"], f32)
    ffn0_w = np.asarray(inputs["L0_ffn_weight"], f32)
    wk = np.asarray(inputs["L1_attn_W_key"], f32)
    wv = np.asarray(inputs["L1_attn_W_value"], f32)
    wo = np.asarray(inputs["L1_attn_W_output"], f32)
    wsh = np.asarray(inputs["L1_attn_W_shift"], f32)
    gate = np.asarray(inputs["L1_attn_gate"], f32)
    tdec = np.asarray(inputs["L1_attn_time_decay"], f32)
    tfir = np.asarray(inputs["L1_attn_time_first"], f32)
    lng = np.asarray(inputs["L1_attn_ln_gamma"], f32)
    lnb = np.asarray(inputs["L1_attn_ln_beta"], f32)
    qkvnw = np.asarray(inputs["L0_attn_qkv_norm_w"], f32)
    projnw = np.asarray(inputs["L0_attn_proj_norm_w"], f32)
    normw = np.asarray(inputs["L0_attn_norm_w"], f32)
    svec = np.asarray(inputs["L0_attn_s"], f32)

    perm_idx = np.arange(MLP).reshape(-1, 128)
    perm_blk = np.concatenate([np.arange(0, 128, 2), np.arange(1, 128, 2)])
    perm_idx = (perm_idx[:, perm_blk] * 0 + perm_idx[:, :1] * 0 +
                (np.arange(MLP).reshape(-1, 128)[:, perm_blk])).reshape(-1)

    def u8view(a):
        a = np.ascontiguousarray(np.asarray(a, np.int32))
        return a.view(np.uint8)[:, ::4].copy()

    gp = u8view(inputs["L1_ffn_gate_W_packed"])[perm_idx]
    up = u8view(inputs["L1_ffn_up_W_packed"])[perm_idx]
    dn = u8view(inputs["L1_ffn_down_W_packed"])
    gs = np.ascontiguousarray(np.asarray(inputs["L1_ffn_gate_scales"], f32))[perm_idx]
    us = np.ascontiguousarray(np.asarray(inputs["L1_ffn_up_scales"], f32))[perm_idx]
    ds = np.ascontiguousarray(np.asarray(inputs["L1_ffn_down_scales"], f32))

    def tiles16(v):
        return np.ascontiguousarray(v.reshape(16, 128).T)

    nvec = np.concatenate([tiles16(qkvnw), tiles16(normw), tiles16(projnw),
                           tiles16(lng), tiles16(lnb)], axis=1)
    gvec = tiles16(gate)
    cstv = np.zeros((128, 8), f32)
    cstv[:, C_M2] = -2.0
    cstv[:, C_M6] = -6.0
    cstv[:, C_MH] = -0.5
    cstv[:, C_PH] = 0.5
    idnb = np.eye(128, dtype=ml_dtypes.bfloat16)
    idnf = np.eye(128, dtype=f32)
    onesM = np.ones((128, 128), f32)
    tt = np.arange(128, dtype=f32)
    diffT = tt[None, :] - tt[:, None]
    triT = (diffT >= 0).astype(f32)
    permM = np.zeros((128, 128), f32)
    sigma = np.concatenate([np.arange(0, 128, 2), np.arange(1, 128, 2)])
    permM[sigma, np.arange(128)] = 1.0
    permM = permM.astype(ml_dtypes.bfloat16)
    niotaR = (-tt)[None, :]
    iotaC = tt[:, None]

    common = dict(
        qkvT=np.ascontiguousarray(qkv_w.T), projT=np.ascontiguousarray(proj_w.T),
        ffn0T=np.ascontiguousarray(ffn0_w.T), shiftT=np.ascontiguousarray(wsh.T),
        wkvT=np.ascontiguousarray(np.concatenate([wk.T, wv.T], axis=1)),
        woutT=np.ascontiguousarray(wo.T),
        nvec=nvec, gvec=gvec, svec=svec[None, :],
        tdvec=np.ascontiguousarray(np.stack([tdec, tfir], axis=1)),
        gp_u8=gp, up_u8=up, dn_u8=dn, gsc=gs, usc=us, dsc=ds,
        cst=cstv, idnb=idnb, idnf=idnf, onesM=onesM, diffT=diffT, triT=triT,
        permM=permM, niotaR=niotaR, iotaC=iotaC,
    )
    in_maps = []
    for c in range(8):
        b, p = c // 2, c % 2
        m = dict(common)
        m["x_fm"] = np.ascontiguousarray(x[b].T)
        pc = np.zeros((128, 2), f32)
        pc[:, 0] = float(p)
        pc[:, 1] = 1.0 - float(p)
        m["pcol"] = pc
        in_maps.append(m)
    return in_maps


def kernel(**inputs):
    in_maps = _host_inputs(inputs)
    nc = build_program()
    res = run_bass_kernel_spmd(nc, in_maps, core_ids=list(range(8)))
    out = np.empty((4, T, D), np.float32)
    for c in range(8):
        b, p = c // 2, c % 2
        out[b, p * TH:(p + 1) * TH, :] = np.asarray(res.results[c]["out"]).T
    return out
